# version 05 (final)
# speedup vs baseline: 1.3094x; 1.0338x over previous
import numpy as np
from contextlib import ExitStack

import concourse.bass as bass
import concourse.mybir as mybir
from concourse.bass_utils import run_bass_kernel_spmd

F32 = mybir.dt.float32
U32 = mybir.dt.uint32
AF = mybir.ActivationFunctionType
ALU = mybir.AluOpType
AX = mybir.AxisListType

P = 128
D = 1024
TLOC = 2048
NT = TLOC // P
NCH = D // P
SQ = 8192
NCORES = 8
GROUPS = [[0, 1, 2, 3], [4, 5, 6, 7]]
NEG = -1.0e30

ALL_BLOCKS = ["l0a", "l0f", "l1a", "l1f", "l2a", "l2f", "l3a", "l3f"]
NDBG = 0
LAUNCH_PLAN = [ALL_BLOCKS]


def host_consts(core_id):
    c = {}
    c["c_ident"] = np.eye(P, dtype=np.float32)
    e32 = np.zeros((32, D), np.float32)
    for b in range(32):
        e32[b, b * 32:(b + 1) * 32] = 1.0
    c["c_e32"] = e32
    sel = np.zeros((P, D), np.float32)
    for p in range(P):
        for b in range(16):
            sel[p, b * 64 + (p % 64)] = 1.0
    c["c_sel"] = sel
    ind8 = np.zeros((P, P), np.float32)
    for cc in range(8):
        for p in range(P):
            ind8[p, cc * 16 + 2 * cc + p // 64] = 1.0
    c["c_ind8"] = ind8
    rank = core_id % 4
    permT = np.zeros((16, 4), np.float32)
    for i in range(4):
        permT[4 * rank + i, i] = 1.0
    c["c_permT"] = permT
    u = np.arange(1280)[None, :]
    pp = np.arange(P)[:, None]
    c["c_causal"] = np.where((u - 384) >= pp, 0.0, NEG).astype(np.float32)[:, :896]
    c["c_win"] = np.where(((u - 384 - pp) >= 0) & ((u - 384 - pp) < 128), 0.0, NEG).astype(np.float32)[:, :1024]
    c["c_iota32"] = np.tile(np.arange(32, dtype=np.float32), (P, 1))
    c["c_ones"] = np.ones((1, 128), np.float32)
    pm = np.zeros((1, 4), np.float32)
    if rank > 0:
        pm[0, rank - 1] = 1.0
    c["c_predmask"] = np.tile(pm, (P, 1)).astype(np.float32)
    c["c_eps"] = np.tile(np.array([[1e-8, 1e-6, 0.0, 0.0]], np.float32), (P, 1))
    return c


CONST_SHAPES = {
    "c_ident": (P, P), "c_e32": (32, D), "c_sel": (P, D), "c_ind8": (P, P),
    "c_permT": (16, 4), "c_causal": (P, 896), "c_win": (P, 1024),
    "c_iota32": (P, 32), "c_ones": (1, 128), "c_predmask": (P, 4), "c_eps": (P, 4),
}


class PB:
    def __init__(self, nc, block, es):
        self.nc = nc
        self.block = block
        self.es = es
        self.sems = {}
        self.c = {}
        for name in ["L", "L2", "ST", "T", "V", "S", "G", "CC"]:
            self.sems[name] = es.enter_context(nc.semaphore("sem_" + name))
            self.c[name] = 0
        self.sb = {}
        self.ps = {}
        self.dbg_ap = None

    def sbuf(self, name, shape, dtype=F32):
        t = self.es.enter_context(self.nc.sbuf_tensor(name, list(shape), dtype))
        self.sb[name] = t
        return t

    def psum(self, name, shape, dtype=F32):
        t = self.es.enter_context(self.nc.psum_tensor(name, list(shape), dtype))
        self.ps[name] = t
        return t

    def inc(self, instr, s, k=1):
        instr.then_inc(self.sems[s], k)
        self.c[s] += k
        return self.c[s]

    def dma(self, eng, out, in_, s="ST"):
        return self.inc(eng.dma_start(out=out, in_=in_), s, 16)

    def w(self, eng, s, val=None):
        eng.wait_ge(self.sems[s], self.c[s] if val is None else val)

    def dbg(self, idx, src_ap, wait=("V", "T", "S")):
        if self.dbg_ap is None or idx >= NDBG:
            return

        @self.block.gpsimd
        def _(g):
            for s in wait:
                self.w(g, s)
            self.dma(g, self.dbg_ap[idx][0:src_ap.shape[0], 0:src_ap.shape[-1]], src_ap, s="ST")


def alloc_common(pb):
    pb.sbuf("x_res", (P, NT * D))
    pb.sbuf("bufA", (P, 16384))
    pb.sbuf("bufB", (P, 8192))
    pb.sbuf("bufC", (P, 4096))
    pb.sbuf("bufS", (P, 2048))
    pb.sbuf("cols", (P, 512))
    pb.sbuf("u8idx", (P, 8), U32)
    pb.sbuf("gcols", (P, 1024))
    for n, shp in CONST_SHAPES.items():
        pb.sbuf(n, shp)
    pb.psum("PS", (P, 4096))


def emit_barrier(pb):
    for ename in ["tensor", "vector", "scalar", "gpsimd", "sync"]:
        @getattr(pb.block, ename)
        def _(e):
            for s in ["T", "V", "S", "ST", "L", "CC", "G"]:
                pb.w(e, s)


def emit_load_consts(pb, io):
    @pb.block.sync
    def _(sync):
        for n in CONST_SHAPES:
            pb.dma(sync, pb.sb[n][:], io[n], s="L")


def emit_load_x(pb, io):
    @pb.block.sync
    def _(sync):
        pb.dma(sync, pb.sb["x_res"][:].rearrange("p (t f) -> p t f", f=D),
               io["x_in"].rearrange("(t p) f -> p t f", p=P), s="L")
    lx = pb.c["L"]

    @pb.block.vector
    def _(v):
        pb.w(v, "L", lx)

    @pb.block.tensor
    def _(t):
        pb.w(t, "L", lx)


def emit_store_x(pb, io):
    @pb.block.gpsimd
    def _(g):
        pb.w(g, "V")
        pb.w(g, "S")
        pb.w(g, "T")
        pb.dma(g, io["x_out"].rearrange("(t p) f -> p t f", p=P),
               pb.sb["x_res"][:].rearrange("p (t f) -> p t f", f=D), s="ST")
        pb.w(g, "ST")


def frag_ln(pb, xsrc, dst, wait_specs=()):
    cols = pb.sb["cols"]

    @pb.block.vector
    def _(v):
        for s, val in wait_specs:
            pb.w(v, s, val)
        pb.inc(v.bn_stats(out=cols[:, 0:6], in_=xsrc[:, 0:512]), "V")
        pb.inc(v.bn_stats(out=cols[:, 6:12], in_=xsrc[:, 512:1024]), "V")
        pb.w(v, "V")
        pb.inc(v.bn_aggr(out=cols[:, 12:14], in_=cols[:, 0:12]), "V")
        pb.w(v, "V")
        pb.inc(v.tensor_scalar(out=cols[:, 14:15], in0=cols[:, 13:14],
                               scalar1=1e-5, scalar2=None, op0=ALU.add, op1=ALU.bypass), "V")

    @pb.block.scalar
    def _(s):
        pb.w(s, "V")
        pb.inc(s.activation(out=cols[:, 15:16], in_=cols[:, 14:15], func=AF.Sqrt), "S")

    @pb.block.vector
    def _(v):
        pb.w(v, "S")
        pb.inc(v.reciprocal(out=cols[:, 16:17], in_=cols[:, 15:16]), "V")
        pb.w(v, "V")
        pb.inc(v.tensor_scalar(out=dst, in0=xsrc, scalar1=cols[:, 12:13],
                               scalar2=cols[:, 16:17], op0=ALU.subtract, op1=ALU.mult), "V")


def frag_transpose_to(pb, src, dst_ap_fn, nchunks, ps_off=3072):
    PS = pb.ps["PS"]
    ident = pb.sb["c_ident"]

    @pb.block.tensor
    def _(t):
        pb.w(t, "V")
        for c in range(nchunks):
            pb.inc(t.transpose(PS[:, ps_off + c * 128: ps_off + (c + 1) * 128],
                               src[:, c * 128:(c + 1) * 128], ident[:]), "T")

    @pb.block.vector
    def _(v):
        pb.w(v, "T")
        for c in range(nchunks):
            pb.inc(v.tensor_copy(dst_ap_fn(c), PS[:, ps_off + c * 128: ps_off + (c + 1) * 128]), "V")


def emit_l0a(pb, io):
    block = pb.block
    xr = pb.sb["x_res"]
    bufA, bufB, bufC, bufS, cols = (pb.sb[k] for k in ["bufA", "bufB", "bufC", "bufS", "cols"])
    PS = pb.ps["PS"]
    ones = pb.sb["c_ones"]
    bank = lambda i: PS[:, i * 512:(i + 1) * 512]

    G = bufA[:, 0:8192]
    Wk_sb = bufA[:, 8192:16384]
    W2_sb = bufB[:, 0:8192]
    lnt = bufS[:, 0:1024]
    lnT = bufC[:, 0:1024]
    qt_sb = bufC[:, 1024:2048]
    kt_sb = bufC[:, 2048:3072]
    gm = bufC[:, 3072:4096]
    red = cols[:, 32:48]

    v_agin, v_agout = io["d_agin"], io["d_agout"]
    en_in, en_out = io["d_enin"], io["d_enout"]
    attn_d = io["d_attn"]

    @block.sync
    def _(sync):
        pb.dma(sync, W2_sb[:].rearrange("p (c f) -> p c f", f=D),
               io["L0_attn_Wq"].rearrange("(c p) f -> p c f", p=P), s="L")
        pb.dma(sync, Wk_sb[:].rearrange("p (c f) -> p c f", f=D),
               io["L0_attn_Wk"].rearrange("(c p) f -> p c f", p=P), s="L")
    L_w1 = pb.c["L"]

    for ti in range(NT):
        xsrc = xr[:, ti * D:(ti + 1) * D]
        frag_ln(pb, xsrc, lnt, wait_specs=(("L", L_w1),) if ti == 0 else ())
        frag_transpose_to(pb, lnt, lambda c: lnT[:, c * 128:(c + 1) * 128], NCH)

        if ti == 0:
            pb.dbg(0, lnt)
            pb.dbg(1, lnT)
        @block.tensor
        def _(t, ti=ti):
            pb.w(t, "V")
            pb.w(t, "ST")
            for half in range(2):
                for c in range(NCH):
                    pb.inc(t.matmul(bank(half), lnT[:, c * 128:(c + 1) * 128],
                                    W2_sb[:, c * D + half * 512: c * D + (half + 1) * 512],
                                    start=(c == 0), stop=(c == NCH - 1)), "T")
            for half in range(2):
                for c in range(NCH):
                    pb.inc(t.matmul(bank(2 + half), lnT[:, c * 128:(c + 1) * 128],
                                    Wk_sb[:, c * D + half * 512: c * D + (half + 1) * 512],
                                    start=(c == 0), stop=(c == NCH - 1)), "T")

        @block.vector
        def _(v, ti=ti):
            pb.w(v, "T")
            pb.inc(v.tensor_copy(qt_sb[:], PS[:, 0:1024]), "V")
            pb.inc(v.tensor_copy(kt_sb[:], PS[:, 1024:2048]), "V")
        if ti == 0:
            pb.dbg(2, qt_sb[:])
            pb.dbg(3, kt_sb[:])

        for c in range(NCH):
            @block.tensor
            def _(t, ti=ti, c=c):
                pb.w(t, "V")
                for half in range(2):
                    pb.inc(t.matmul(bank(4 + half), qt_sb[:, c * 128:(c + 1) * 128],
                                    kt_sb[:, half * 512:(half + 1) * 512],
                                    start=True, stop=True), "T")

            @block.vector
            def _(v, ti=ti, c=c):
                pb.w(v, "T")
                dst = G[:, c * 1024:(c + 1) * 1024]
                if ti == 0:
                    pb.inc(v.tensor_copy(dst, PS[:, 2048:3072]), "V")
                else:
                    pb.inc(v.tensor_tensor(out=dst, in0=PS[:, 2048:3072], in1=dst, op=ALU.add), "V")

    selc, ind8 = pb.sb["c_sel"], pb.sb["c_ind8"]
    en_sb = cols[:, 48:64]
    for c in range(NCH):
        @block.vector
        def _(v, c=c):
            pb.inc(v.tensor_tensor(out=gm, in0=G[:, c * 1024:(c + 1) * 1024], in1=selc[:], op=ALU.mult), "V")
            pb.w(v, "V")
            pb.inc(v.tensor_reduce(out=red, in_=gm.rearrange("p (b e) -> p b e", e=64),
                                   axis=AX.X, op=ALU.add), "V")

        @block.tensor
        def _(t, c=c):
            pb.w(t, "V")
            pb.inc(t.matmul(PS[0:16, 0:16], ind8[:, c * 16:(c + 1) * 16], red,
                            start=(c == 0), stop=(c == NCH - 1)), "T")

    pb.dbg(4, G[:, 0:1024])
    @block.vector
    def _(v):
        pb.w(v, "T")
        pb.inc(v.tensor_copy(en_sb[0:16, :], PS[0:16, 0:16]), "V")
    pb.dbg(5, en_sb[0:16, :])

    @block.gpsimd
    def _(g):
        pb.w(g, "V")
        pb.dma(g, en_in, en_sb[0:16, :], s="ST")
        pb.w(g, "ST")
        pb.inc(g.collective_compute("AllReduce", ALU.add, replica_groups=GROUPS,
                                    ins=[en_in], outs=[en_out]), "CC")
        pb.w(g, "CC")
        pb.dma(g, en_sb[0:16, :], en_out, s="ST")
    ST_en = pb.c["ST"]
    pb.dbg(6, en_sb[0:16, :], wait=("ST",))

    permT = pb.sb["c_permT"]
    attn_bc = cols[:, 64:128]

    @block.vector
    def _(v):
        pb.w(v, "ST", ST_en)
        pb.inc(v.tensor_reduce(out=cols[0:16, 0:1], in_=en_sb[0:16, :], axis=AX.X, op=ALU.max), "V")
        pb.w(v, "V")
        pb.inc(v.tensor_scalar(out=cols[0:16, 1:2], in0=cols[0:16, 0:1],
                               scalar1=-0.125, scalar2=None, op0=ALU.mult, op1=ALU.bypass), "V")

    @block.scalar
    def _(s):
        pb.w(s, "V")
        pb.inc(s.activation(out=cols[0:16, 16:32], in_=en_sb[0:16, :], func=AF.Exp,
                            bias=cols[0:16, 1:2], scale=0.125, accum_out=cols[0:16, 2:3]), "S")

    @block.vector
    def _(v):
        pb.w(v, "S")
        pb.inc(v.reciprocal(out=cols[0:16, 3:4], in_=cols[0:16, 2:3]), "V")
        pb.w(v, "V")
        pb.inc(v.tensor_scalar(out=en_sb[0:16, :], in0=cols[0:16, 16:32],
                               scalar1=cols[0:16, 3:4], scalar2=None, op0=ALU.mult, op1=ALU.bypass), "V")

    @block.tensor
    def _(t):
        pb.w(t, "V")
        pb.inc(t.matmul(PS[0:4, 0:16], permT[:], en_sb[0:16, :], start=True, stop=True), "T")

    @block.vector
    def _(v):
        pb.w(v, "T")
        pb.inc(v.tensor_copy(cols[0:4, 4:20], PS[0:4, 0:16]), "V")

    @block.gpsimd
    def _(g):
        pb.w(g, "V")
        pb.dma(g, attn_d, cols[0:4, 4:20], s="ST")
    ST_attn = pb.c["ST"]

    @block.sync
    def _(sync):
        pb.w(sync, "ST", ST_attn)
        pb.dma(sync, cols[0:1, 20:84], attn_d.rearrange("a b -> (a b)").unsqueeze(0), s="L")
    L_attn = pb.c["L"]

    @block.tensor
    def _(t):
        pb.w(t, "L", L_attn)
        pb.inc(t.matmul(PS[:, 0:64], ones[0:1, 0:128], cols[0:1, 20:84], start=True, stop=True), "T")

    @block.vector
    def _(v):
        pb.w(v, "T")
        pb.inc(v.tensor_copy(attn_bc, PS[:, 0:64]), "V")
    V_attnbc = pb.c["V"]
    pb.dbg(7, en_sb[0:16, :])
    pb.dbg(8, attn_bc)

    @block.sync
    def _(sync):
        pb.w(sync, "T")
        pb.dma(sync, W2_sb[:].rearrange("p (c f) -> p c f", f=D),
               io["L0_attn_Wv"].rearrange("(c p) f -> p c f", p=P), s="L")
    L_wv = pb.c["L"]

    for ti in range(NT):
        xsrc = xr[:, ti * D:(ti + 1) * D]
        frag_ln(pb, xsrc, lnt, wait_specs=(("L", L_wv),) if ti == 0 else ())
        frag_transpose_to(pb, lnt, lambda c: lnT[:, c * 128:(c + 1) * 128], NCH)

        @block.tensor
        def _(t, ti=ti):
            pb.w(t, "V")
            for half in range(2):
                for c in range(NCH):
                    pb.inc(t.matmul(bank(half), lnT[:, c * 128:(c + 1) * 128],
                                    W2_sb[:, c * D + half * 512: c * D + (half + 1) * 512],
                                    start=(c == 0), stop=(c == NCH - 1)), "T")

        @block.vector
        def _(v, ti=ti):
            pb.w(v, "T")
            pb.inc(v.tensor_copy(qt_sb[:], PS[:, 0:1024]), "V")

        @block.gpsimd
        def _(g, ti=ti):
            pb.w(g, "V")
            pb.dma(g, v_agin[ti * P:(ti + 1) * P, :], qt_sb[:], s="ST")

    @block.gpsimd
    def _(g):
        pb.w(g, "ST")
        pb.inc(g.collective_compute("AllGather", ALU.bypass, replica_groups=GROUPS,
                                    ins=[v_agin], outs=[v_agout]), "CC")
    CC_v = pb.c["CC"]

    @block.sync
    def _(sync):
        pb.w(sync, "T")
        pb.dma(sync, W2_sb[:].rearrange("p (c f) -> p c f", f=D),
               io["L0_attn_Wo"].rearrange("(c p) f -> p c f", p=P), s="L")
        pb.dma(sync, bufC[0:1, 1024:1024 + D], io["L0_attn_bo"].unsqueeze(0), s="L")
    L_wo = pb.c["L"]
    bo_row = bufC[0:1, 1024:1024 + D]

    acc0 = bufS[:, 0:1024]
    acc1 = bufS[:, 1024:2048]
    rfm = bufC[:, 0:1024]
    Vm = bufA[:, 0:16384]

    for mt in range(4):
        @block.sync
        def _(sync, mt=mt):
            if mt == 0:
                pb.w(sync, "CC", CC_v)
            pb.w(sync, "V")
            pb.dma(sync, Vm[:].rearrange("p (u f) -> p u f", f=D),
                   v_agout[mt * TLOC:(mt + 1) * TLOC, :].rearrange("(p u) f -> p u f", u=16), s="L")
        L_vm = pb.c["L"]

        for al in range(4):
            @block.vector
            def _(v, mt=mt, al=al):
                pb.w(v, "L", L_vm)
                pb.w(v, "V", V_attnbc)
                for b in range(16):
                    src = Vm[:].rearrange("p (u f) -> p u f", f=D)[:, :, b * 64:(b + 1) * 64]
                    dst, prev = (acc0, acc1) if b % 2 == 0 else (acc1, acc0)
                    sc = attn_bc[:, al * 16 + b: al * 16 + b + 1]
                    if b == 0:
                        pb.inc(v.tensor_scalar(out=dst.rearrange("p (u f) -> p u f", f=64),
                                               in0=src, scalar1=sc, scalar2=None,
                                               op0=ALU.mult, op1=ALU.bypass), "V")
                    else:
                        pb.w(v, "V")
                        pb.inc(v.scalar_tensor_tensor(out=dst.rearrange("p (u f) -> p u f", f=64),
                                                      in0=src, scalar=sc,
                                                      in1=prev.rearrange("p (u f) -> p u f", f=64),
                                                      op0=ALU.mult, op1=ALU.add), "V")

            if mt == 0 and al == 0:
                pb.dbg(9, acc1)
            frag_transpose_to(pb, acc1, lambda c: rfm[:, c * 128:(c + 1) * 128], NCH)

            @block.tensor
            def _(t, mt=mt, al=al):
                pb.w(t, "V")
                if mt == 0 and al == 0:
                    pb.w(t, "L", L_wo)
                for half in range(2):
                    for c in range(NCH):
                        pb.inc(t.matmul(bank(half), rfm[:, c * 128:(c + 1) * 128],
                                        W2_sb[:, c * D + half * 512: c * D + (half + 1) * 512],
                                        start=(c == 0), stop=False), "T")
                    pb.inc(t.matmul(bank(half), ones[0:1, 0:128],
                                    bo_row[0:1, half * 512:(half + 1) * 512],
                                    start=False, stop=True), "T")

            @block.vector
            def _(v, mt=mt, al=al):
                pb.w(v, "T")
                xt = xr[:, (al * 4 + mt) * D:(al * 4 + mt + 1) * D]
                pb.inc(v.tensor_tensor(out=xt, in0=PS[:, 0:1024], in1=xt, op=ALU.add), "V")


def emit_l0f(pb, io):
    block = pb.block
    xr = pb.sb["x_res"]
    bufA, bufB, bufC, bufS, cols = (pb.sb[k] for k in ["bufA", "bufB", "bufC", "bufS", "cols"])
    PS = pb.ps["PS"]
    ident = pb.sb["c_ident"]
    e32 = pb.sb["c_e32"]
    ones = pb.sb["c_ones"]
    bank = lambda i: PS[:, i * 512:(i + 1) * 512]

    wdT = bufA[:, 0:8192]
    wraw = bufA[:, 8192:16384]
    lnt = bufS[:, 0:1024]
    xdT = bufC[:, 0:1024]
    sctr = bufC[:, 1024:1152]
    sctr2 = bufC[:, 1152:1280]
    sc32 = cols[:, 0:32]
    rc32 = cols[:, 32:64]
    sl = lambda i: bufB[:, i * 1024:(i + 1) * 1024]
    C1, C2, C3 = 1572864.0, 3145728.0, 6291456.0

    def frag_quant(src, dst):
        @block.vector
        def _(v):
            pb.inc(v.tensor_reduce(out=sc32[:], in_=src.rearrange("p (b e) -> p b e", e=32),
                                   axis=AX.X, op=ALU.max, apply_absolute_value=True), "V")
            pb.w(v, "V")
            pb.inc(v.tensor_scalar(out=sc32[:], in0=sc32[:], scalar1=1.0 / 7.5,
                                   scalar2=1e-30, op0=ALU.mult, op1=ALU.max), "V")
            pb.w(v, "V")
            pb.inc(v.reciprocal(out=rc32[:], in_=sc32[:]), "V")

        @block.tensor
        def _(t):
            pb.w(t, "V")
            pb.inc(t.transpose(PS[0:32, 3072:3200], rc32[:], ident[:]), "T")
            pb.inc(t.transpose(PS[0:32, 3200:3328], sc32[:], ident[:]), "T")

        @block.vector
        def _(v):
            pb.w(v, "T")
            pb.inc(v.tensor_copy(sctr[0:32, 0:128], PS[0:32, 3072:3200]), "V")
            pb.inc(v.tensor_copy(sctr2[0:32, 0:128], PS[0:32, 3200:3328]), "V")

        @block.tensor
        def _(t):
            pb.w(t, "V")
            pb.inc(t.matmul(PS[:, 2048:2560], sctr[0:32, 0:128], e32[:, 0:512], start=True, stop=True), "T")
            pb.inc(t.matmul(PS[:, 2560:3072], sctr[0:32, 0:128], e32[:, 512:1024], start=True, stop=True), "T")

        @block.vector
        def _(v):
            pb.w(v, "T")
            pb.inc(v.tensor_copy(sl(6), PS[:, 2048:3072]), "V")
            pb.w(v, "V")
            pb.inc(v.tensor_tensor(out=sl(0), in0=src, in1=sl(6), op=ALU.mult), "V")

        @block.tensor
        def _(t):
            pb.w(t, "V")
            pb.inc(t.matmul(PS[:, 2048:2560], sctr2[0:32, 0:128], e32[:, 0:512], start=True, stop=True), "T")
            pb.inc(t.matmul(PS[:, 2560:3072], sctr2[0:32, 0:128], e32[:, 512:1024], start=True, stop=True), "T")

        @block.vector
        def _(v):
            pb.w(v, "T")
            pb.inc(v.tensor_copy(sl(5), PS[:, 2048:3072]), "V")
            pb.inc(v.tensor_tensor(out=sl(1), in0=sl(0), in1=sl(0), op=ALU.mult), "V")
            pb.inc(v.tensor_scalar(out=sl(2), in0=sl(0), scalar1=C1, scalar2=C1,
                                   op0=ALU.add, op1=ALU.subtract), "V")
            pb.inc(v.tensor_scalar(out=sl(3), in0=sl(0), scalar1=C2, scalar2=C2,
                                   op0=ALU.add, op1=ALU.subtract), "V")
            pb.inc(v.tensor_scalar(out=sl(4), in0=sl(0), scalar1=C3, scalar2=C3,
                                   op0=ALU.add, op1=ALU.subtract), "V")
            pb.w(v, "V")
            pb.inc(v.tensor_scalar(out=sl(7), in0=sl(1), scalar1=16.0, scalar2=None,
                                   op0=ALU.is_lt, op1=ALU.bypass), "V")
            pb.inc(v.tensor_scalar(out=sl(1), in0=sl(1), scalar1=4.0, scalar2=None,
                                   op0=ALU.is_lt, op1=ALU.bypass), "V")
            pb.inc(v.tensor_tensor(out=sl(2), in0=sl(2), in1=sl(3), op=ALU.subtract), "V")
            pb.w(v, "V")
            pb.inc(v.tensor_tensor(out=sl(3), in0=sl(3), in1=sl(4), op=ALU.subtract), "V")
            pb.inc(v.tensor_tensor(out=sl(2), in0=sl(2), in1=sl(1), op=ALU.mult), "V")
            pb.w(v, "V")
            pb.inc(v.tensor_tensor(out=sl(3), in0=sl(3), in1=sl(7), op=ALU.mult), "V")
            pb.w(v, "V")
            pb.inc(v.tensor_tensor(out=sl(4), in0=sl(4), in1=sl(3), op=ALU.add), "V")
            pb.w(v, "V")
            pb.inc(v.tensor_tensor(out=sl(4), in0=sl(4), in1=sl(2), op=ALU.add), "V")
            pb.w(v, "V")
            pb.inc(v.tensor_tensor(out=dst, in0=sl(4), in1=sl(5), op=ALU.mult), "V")

    @block.sync
    def _(sync):
        pb.dma(sync, wraw[:].rearrange("p (c f) -> p c f", f=D),
               io["L0_ffn_weight"].rearrange("(c p) f -> p c f", p=P), s="L")
        pb.dma(sync, bufC[0:1, 1280:1280 + D], io["L0_ffn_bias"].unsqueeze(0), s="L")
    L_w = pb.c["L"]

    for c in range(NCH):
        @block.vector
        def _(v, c=c):
            if c == 0:
                pb.w(v, "L", L_w)
        src = wraw[:, c * 1024:(c + 1) * 1024]
        frag_quant(src, src)
        frag_transpose_to(pb, src, lambda s2, c=c: wdT[:, s2 * 1024 + c * 128: s2 * 1024 + (c + 1) * 128], NCH)

    bo_row = bufC[0:1, 1280:1280 + D]
    for ti in range(NT):
        xsrc = xr[:, ti * D:(ti + 1) * D]
        frag_ln(pb, xsrc, lnt)
        frag_quant(lnt, lnt)
        frag_transpose_to(pb, lnt, lambda c: xdT[:, c * 128:(c + 1) * 128], NCH)

        @block.tensor
        def _(t, ti=ti):
            pb.w(t, "V")
            for half in range(2):
                for s2 in range(NCH):
                    pb.inc(t.matmul(bank(half), xdT[:, s2 * 128:(s2 + 1) * 128],
                                    wdT[:, s2 * 1024 + half * 512: s2 * 1024 + (half + 1) * 512],
                                    start=(s2 == 0), stop=False), "T")
                pb.inc(t.matmul(bank(half), ones[0:1, 0:128],
                                bo_row[0:1, half * 512:(half + 1) * 512],
                                start=False, stop=True), "T")

        @block.vector
        def _(v, ti=ti):
            pb.w(v, "T")
            xt = xr[:, ti * D:(ti + 1) * D]
            pb.inc(v.tensor_tensor(out=xt, in0=PS[:, 0:1024], in1=xt, op=ALU.add), "V")


def emit_l1f(pb, io):
    import os
    SKIP = os.environ.get("L1F_SKIP", "")
    block = pb.block
    xr = pb.sb["x_res"]
    bufA, bufB, bufC, bufS, cols = (pb.sb[k] for k in ["bufA", "bufB", "bufC", "bufS", "cols"])
    PS = pb.ps["PS"]
    ones = pb.sb["c_ones"]
    bank = lambda i: PS[:, i * 512:(i + 1) * 512]

    WcT = bufA[:, 0:8192]
    g1l = bufC[0:4, 2048:3072]
    g2l = bufC[0:4, 3072:3328]
    lnt = bufS[:, 0:1024]
    lnT = bufC[:, 0:1024]
    bias_row = bufC[0:1, 1024:2048]

    @block.sync
    def _(sync):
        pb.dma(sync, bias_row[0:1, 0:512], io["L1_ffn_bias_real"].unsqueeze(0), s="L")
        pb.dma(sync, bias_row[0:1, 512:1024], io["L1_ffn_bias_imag"].unsqueeze(0), s="L")
    L_b = pb.c["L"]

    for phase, (g1n, g2n) in enumerate([("L1_ffn_G1_R", "L1_ffn_G2_R"),
                                        ("L1_ffn_G1_I", "L1_ffn_G2_I")]):
        @block.sync
        def _(sync, g1n=g1n, g2n=g2n):
            pb.w(sync, "T")
            pb.dma(sync, g1l, io[g1n], s="L")
            pb.dma(sync, g2l, io[g2n], s="L")
        L_g = pb.c["L"]

        vbase = pb.c["V"]
        for b in range(0 if "bloop" in SKIP else 32):
            @block.tensor
            def _(t, b=b):
                if b == 0:
                    pb.w(t, "L", L_g)
                    pb.w(t, "V")
                if b >= 4:
                    pb.w(t, "V", vbase + b - 3)
                pb.inc(t.matmul(PS[0:32, 2048 + (b % 4) * 512: 2048 + (b % 4) * 512 + 256],
                                g1l[:, b * 32:(b + 1) * 32], g2l,
                                start=True, stop=True), "T")

            @block.vector
            def _(v, b=b):
                if b == 0:
                    pb.w(v, "ST")
                pb.w(v, "T")
                dstv = bufB[0:32, :].rearrange("p (g x) -> p g x", x=512)[:, :, b * 16:(b + 1) * 16]
                pb.inc(v.tensor_copy(dstv,
                                     PS[0:32, 2048 + (b % 4) * 512: 2048 + (b % 4) * 512 + 256].rearrange(
                                         "p (g e) -> p g e", e=16)), "V")

        if phase == 1:
            @block.vector
            def _(v):
                pb.w(v, "V")
                pb.inc(v.tensor_scalar(out=bufB[32:64, 0:8192], in0=bufB[0:32, 0:8192],
                                       scalar1=-1.0, scalar2=None, op0=ALU.mult, op1=ALU.bypass), "V")

        quads = ([(0, 0, 0), (1, 1, 0)] if phase == 0 else [(0, 1, 0), (1, 0, 32)]) if "scat" not in SKIP else []
        @block.gpsimd
        def _(g, quads=quads):
            pb.w(g, "V")
            for (qr, qc, srow) in quads:
                for k in range(4):
                    src = bass.AP(bufB.ap().tensor, (8 * k + srow) * 8192,
                                  [[8192, 8], [512, 16], [1, 512]])
                    dst = bass.AP(bufA.ap().tensor,
                                  (qr * 4 + k) * 1024 + qc * 512,
                                  [[16384, 128], [1, 512]])
                    pb.dma(g, dst, src, s="ST")
    ST_w = pb.c["ST"]

    for c8 in range(NCH):
        pb.dbg(c8, WcT[:, c8 * 1024:(c8 + 1) * 1024], wait=("V", "ST"))
    for ti in range(0 if "main" in SKIP else NT):
        xsrc = xr[:, ti * D:(ti + 1) * D]
        frag_ln(pb, xsrc, lnt, wait_specs=(("ST", ST_w), ("L", L_b)) if ti == 0 else ())
        frag_transpose_to(pb, lnt, lambda c: lnT[:, c * 128:(c + 1) * 128], NCH)

        @block.tensor
        def _(t, ti=ti):
            pb.w(t, "V")
            if ti == 0:
                pb.w(t, "ST", ST_w)
            for half in range(2):
                for c in range(NCH):
                    pb.inc(t.matmul(bank(half), lnT[:, c * 128:(c + 1) * 128],
                                    WcT[:, c * 1024 + half * 512: c * 1024 + (half + 1) * 512],
                                    start=(c == 0), stop=False), "T")
                pb.inc(t.matmul(bank(half), ones[0:1, 0:128],
                                bias_row[0:1, half * 512:(half + 1) * 512],
                                start=False, stop=True), "T")

        @block.vector
        def _(v, ti=ti):
            pb.w(v, "T")
            xt = xr[:, ti * D:(ti + 1) * D]
            pb.inc(v.tensor_tensor(out=xt, in0=PS[:, 0:1024], in1=xt, op=ALU.add), "V")


def emit_l2f(pb, io):
    block = pb.block
    xr = pb.sb["x_res"]
    bufA, bufB, bufC, bufS, cols = (pb.sb[k] for k in ["bufA", "bufB", "bufC", "bufS", "cols"])
    PS = pb.ps["PS"]
    ident = pb.sb["c_ident"]
    ones = pb.sb["c_ones"]
    bank = lambda i: PS[:, i * 512:(i + 1) * 512]

    TG = 256
    NG = TLOC // TG
    h_fm = bufA[:, 0:8192]
    g_fm = bufA[:, 8192:16384]
    lnTg = bufS[:, 0:2048]
    lnt = bufC[:, 0:1024]
    tmp = bufC[:, 1024:2048]
    s3bc = bufC[:, 2048:3072]
    wbuf = bufB
    scl = cols

    @block.sync
    def _(sync):
        for i, wn in enumerate(["L2_ffn_w1_s", "L2_ffn_w2_s", "L2_ffn_h1_s", "L2_ffn_h2_s"]):
            pb.dma(sync, bufC[0:32, 3072 + i * 128: 3072 + (i + 1) * 128],
                   io[wn].rearrange("(c p) -> c p", p=128), s="L")
        pb.dma(sync, bufB[0:1, 0:1024], io["L2_ffn_w3_s"].unsqueeze(0), s="L")
    L_s = pb.c["L"]

    @block.tensor
    def _(t):
        pb.w(t, "L", L_s)
        for i in range(4):
            pb.inc(t.transpose(PS[0:128, 3072 + i * 128: 3072 + i * 128 + 32],
                               bufC[0:32, 3072 + i * 128: 3072 + (i + 1) * 128], ident[0:32, 0:32]), "T")
        pb.inc(t.matmul(PS[:, 2048:2560], ones[0:1, 0:128], bufB[0:1, 0:512],
                        start=True, stop=True), "T")
        pb.inc(t.matmul(PS[:, 2560:3072], ones[0:1, 0:128], bufB[0:1, 512:1024],
                        start=True, stop=True), "T")

    @block.vector
    def _(v):
        pb.w(v, "T")
        for i in range(4):
            pb.inc(v.tensor_copy(scl[:, 136 + i * 32:136 + (i + 1) * 32],
                                 PS[0:128, 3072 + i * 128: 3072 + i * 128 + 32]), "V")
        pb.inc(v.tensor_copy(s3bc, PS[:, 2048:3072]), "V")
    V_s = pb.c["V"]

    def stream_w(sync_eng, name, oc, slot, ncol, kdim):
        pb.dma(sync_eng, wbuf[:, slot * 1024: slot * 1024 + (kdim // 128) * ncol].rearrange(
            "p (kc f) -> p kc f", f=ncol),
            io[name][:, oc * ncol:(oc + 1) * ncol].rearrange("(kc p) f -> p kc f", p=128), s="L")

    for g in range(NG):
        for ti2 in range(TG // P):
            ti = g * (TG // P) + ti2
            xsrc = xr[:, ti * D:(ti + 1) * D]
            frag_ln(pb, xsrc, lnt, wait_specs=(("V", V_s),) if g == 0 and ti2 == 0 else ())
            frag_transpose_to(pb, lnt, lambda c, ti2=ti2: lnTg[:, c * TG + ti2 * 128:c * TG + ti2 * 128 + 128], NCH)

        s1_guard = {}
        t_stage = pb.c["T"]
        for oc in range(32):
            slot = oc % 2

            @block.sync
            def _(sync, g=g, oc=oc, slot=slot):
                pb.w(sync, "T", s1_guard.get(oc - 2, t_stage))
                stream_w(sync, "L2_ffn_w1_w", oc, 2 * slot, 128, 1024)
                stream_w(sync, "L2_ffn_w2_w", oc, 2 * slot + 1, 128, 1024)
            L_wc = pb.c["L"]

            @block.tensor
            def _(t, g=g, oc=oc, slot=slot):
                pb.w(t, "L", L_wc)
                pb.w(t, "V")
                pb.w(t, "S")
                for c in range(NCH):
                    pb.inc(t.matmul(PS[:, 0:TG], wbuf[:, 2 * slot * 1024 + c * 128: 2 * slot * 1024 + (c + 1) * 128],
                                    lnTg[:, c * TG:(c + 1) * TG],
                                    start=(c == 0), stop=(c == NCH - 1)), "T")
                for c in range(NCH):
                    pb.inc(t.matmul(PS[:, 512:512 + TG], wbuf[:, (2 * slot + 1) * 1024 + c * 128: (2 * slot + 1) * 1024 + (c + 1) * 128],
                                    lnTg[:, c * TG:(c + 1) * TG],
                                    start=(c == 0), stop=(c == NCH - 1)), "T")
            s1_guard[oc] = pb.c["T"]

            @block.scalar
            def _(s, g=g, oc=oc):
                pb.w(s, "T")
                pb.inc(s.activation(out=tmp[:, 0:TG], in_=PS[:, 0:TG], func=AF.Silu,
                                    scale=scl[:, 136 + oc:137 + oc]), "S")

            @block.vector
            def _(v, g=g, oc=oc):
                pb.w(v, "S")
                pb.inc(v.scalar_tensor_tensor(out=h_fm[:, oc * TG:(oc + 1) * TG],
                                              in0=PS[:, 512:512 + TG], scalar=scl[:, 168 + oc:169 + oc],
                                              in1=tmp[:, 0:TG], op0=ALU.mult, op1=ALU.mult), "V")

        s2_guard = {}
        t_stage = pb.c["T"]
        for oc in range(32):
            @block.sync
            def _(sync, g=g, oc=oc):
                pb.w(sync, "T", s2_guard.get(oc - 2, t_stage))
                stream_w(sync, "L2_ffn_h1_w", oc, (oc % 2) * 4, 128, 4096)
            L_wc = pb.c["L"]

            @block.tensor
            def _(t, g=g, oc=oc):
                pb.w(t, "L", L_wc)
                pb.w(t, "V")
                pb.w(t, "S")
                for c in range(32):
                    pb.inc(t.matmul(PS[:, 0:TG] if oc % 2 == 0 else PS[:, 512:512 + TG],
                                    wbuf[:, (oc % 2) * 4096 + c * 128:(oc % 2) * 4096 + (c + 1) * 128],
                                    h_fm[:, c * TG:(c + 1) * TG],
                                    start=(c == 0), stop=(c == 31)), "T")
            s2_guard[oc] = pb.c["T"]

            @block.scalar
            def _(s, g=g, oc=oc):
                pb.w(s, "T")
                pb.inc(s.activation(out=g_fm[:, oc * TG:(oc + 1) * TG],
                                    in_=PS[:, 0:TG] if oc % 2 == 0 else PS[:, 512:512 + TG],
                                    func=AF.Gelu, scale=scl[:, 200 + oc:201 + oc]), "S")

        s3_guard = {}
        t_stage = pb.c["T"]
        s_stage0 = pb.c["S"]
        for oc in range(32):
            @block.sync
            def _(sync, g=g, oc=oc):
                pb.w(sync, "T", s3_guard.get(oc - 2, t_stage))
                if oc == 0:
                    pb.w(sync, "S", s_stage0)
                stream_w(sync, "L2_ffn_h2_w", oc, (oc % 2) * 4, 128, 4096)
            L_wc = pb.c["L"]

            @block.tensor
            def _(t, g=g, oc=oc):
                pb.w(t, "L", L_wc)
                pb.w(t, "S")
                for c in range(32):
                    pb.inc(t.matmul(PS[:, 0:TG] if oc % 2 == 0 else PS[:, 512:512 + TG],
                                    wbuf[:, (oc % 2) * 4096 + c * 128:(oc % 2) * 4096 + (c + 1) * 128],
                                    g_fm[:, c * TG:(c + 1) * TG],
                                    start=(c == 0), stop=(c == 31)), "T")

            @block.scalar
            def _(s, g=g, oc=oc):
                pb.w(s, "T")
                pb.inc(s.activation(out=h_fm[:, oc * TG:(oc + 1) * TG],
                                    in_=PS[:, 0:TG] if oc % 2 == 0 else PS[:, 512:512 + TG],
                                    func=AF.Gelu, scale=scl[:, 232 + oc:233 + oc]), "S")
            s3_guard[oc] = pb.c["T"]

        for qt4 in range(4):
            @block.sync
            def _(sync, g=g, qt4=qt4):
                pb.w(sync, "T")
                pb.dma(sync, wbuf[:].rearrange("p (kc f) -> p kc f", f=1024),
                       io["L2_ffn_w3_w"][qt4 * 1024:(qt4 + 1) * 1024, :].rearrange(
                           "(kc p) f -> p kc f", p=128), s="L")
            L_wc = pb.c["L"]

            @block.tensor
            def _(t, g=g, qt4=qt4):
                pb.w(t, "L", L_wc)
                pb.w(t, "S")
                if qt4 == 0:
                    pb.w(t, "V")
                for ti2 in range(TG // P):
                    for half in range(2):
                        for c in range(8):
                            cc = qt4 * 8 + c
                            pb.inc(t.matmul(bank(2 + 2 * ti2 + half),
                                            h_fm[:, cc * TG + ti2 * 128: cc * TG + ti2 * 128 + 128],
                                            wbuf[:, c * 1024 + half * 512: c * 1024 + (half + 1) * 512],
                                            start=(qt4 == 0 and c == 0), stop=(qt4 == 3 and c == 7)), "T")

        for ti2 in range(TG // P):
            ti = g * (TG // P) + ti2

            @block.vector
            def _(v, g=g, ti=ti, ti2=ti2):
                pb.w(v, "T")
                pb.inc(v.tensor_tensor(out=tmp, in0=PS[:, 1024 + ti2 * 1024:2048 + ti2 * 1024], in1=s3bc, op=ALU.mult), "V")
                pb.w(v, "V")
                xt = xr[:, ti * D:(ti + 1) * D]
                pb.inc(v.tensor_tensor(out=xt, in0=tmp, in1=xt, op=ALU.add), "V")


def emit_l3f(pb, io):
    block = pb.block
    xr = pb.sb["x_res"]
    bufA, bufB, bufC, bufS, cols = (pb.sb[k] for k in ["bufA", "bufB", "bufC", "bufS", "cols"])
    PS = pb.ps["PS"]
    iota32 = pb.sb["c_iota32"]
    gidx = pb.sb["u8idx"]
    bank = lambda i: PS[:, i * 512:(i + 1) * 512]

    lnT = bufA[:, 0:16384]
    h_fm = bufB[:, 0:8192]
    lnt = bufS[:, 1024:2048]
    tmp = bufS[:, 1024:1536]
    comb = bufS[:, 0:1024]
    gwT = bufC[0:128, 3072:3328]

    @block.sync
    def _(sync):
        pb.dma(sync, gwT[:].rearrange("p (c f) -> p c f", f=32),
               io["L3_ffn_gate_w"].rearrange("(c p) f -> p c f", p=P), s="L")
    L_gw = pb.c["L"]

    for ti in range(NT):
        xsrc = xr[:, ti * D:(ti + 1) * D]
        frag_ln(pb, xsrc, lnt)
        frag_transpose_to(pb, lnt, lambda c, ti=ti: lnT[:, c * 2048 + ti * 128: c * 2048 + (ti + 1) * 128], NCH)

        @block.tensor
        def _(t, ti=ti):
            pb.w(t, "V")
            if ti == 0:
                pb.w(t, "L", L_gw)
            for c in range(NCH):
                pb.inc(t.matmul(PS[:, 0:32], lnT[:, c * 2048 + ti * 128: c * 2048 + (ti + 1) * 128],
                                gwT[:, c * 32:(c + 1) * 32], start=(c == 0), stop=(c == NCH - 1)), "T")

        @block.scalar
        def _(s, ti=ti):
            pb.w(s, "T")
            pb.inc(s.activation(out=cols[:, 300:332], in_=PS[:, 0:32], func=AF.Sigmoid), "S")

        @block.vector
        def _(v, ti=ti):
            pb.w(v, "S")
            pb.inc(v.tensor_reduce(out=cols[:, 332:340],
                                   in_=cols[:, 300:332].rearrange("p (g e) -> p g e", e=4),
                                   axis=AX.X, op=ALU.add), "V")
            pb.w(v, "V")
            pb.inc(v.max(cols[:, 340:348], cols[:, 332:340]), "V")
            pb.w(v, "V")
            pb.inc(v.max_index(gidx[:, 0:8], cols[:, 340:348], cols[:, 332:340]), "V")

        @block.scalar
        def _(s, ti=ti):
            pb.w(s, "V")
            pb.inc(s.copy(out=cols[:, 348:356], in_=gidx[:, 0:8]), "S")

        @block.vector
        def _(v, ti=ti):
            pb.w(v, "S")
            for k in range(4):
                if k == 0:
                    pb.inc(v.tensor_scalar(out=cols[:, 374:382], in0=pb.sb["c_iota32"][:, 0:8],
                                           scalar1=cols[:, 348 + k:349 + k], scalar2=None,
                                           op0=ALU.is_equal, op1=ALU.bypass), "V")
                else:
                    pb.inc(v.tensor_scalar(out=cols[:, 382:390], in0=pb.sb["c_iota32"][:, 0:8],
                                           scalar1=cols[:, 348 + k:349 + k], scalar2=None,
                                           op0=ALU.is_equal, op1=ALU.bypass), "V")
                    pb.w(v, "V")
                    pb.inc(v.tensor_tensor(out=cols[:, 374:382], in0=cols[:, 374:382],
                                           in1=cols[:, 382:390], op=ALU.add), "V")
                    pb.w(v, "V")
            for g in range(8):
                pb.inc(v.tensor_scalar(out=cols[:, 390 + g * 4:394 + g * 4],
                                       in0=cols[:, 300 + g * 4:304 + g * 4],
                                       scalar1=cols[:, 374 + g:375 + g], scalar2=None,
                                       op0=ALU.mult, op1=ALU.bypass), "V")
            pb.w(v, "V")
            pb.inc(v.max(cols[:, 422:430], cols[:, 390:422]), "V")
            pb.w(v, "V")
            pb.inc(v.max_index(gidx[:, 0:8], cols[:, 422:430], cols[:, 390:422]), "V")

        @block.scalar
        def _(s, ti=ti):
            pb.w(s, "V")
            pb.inc(s.copy(out=cols[:, 348:356], in_=gidx[:, 0:8]), "S")

        @block.vector
        def _(v, ti=ti):
            pb.w(v, "S")
            pb.inc(v.tensor_reduce(out=cols[:, 430:431], in_=cols[:, 422:430], axis=AX.X, op=ALU.add), "V")
            pb.w(v, "V")
            pb.inc(v.tensor_scalar(out=cols[:, 430:431], in0=cols[:, 430:431], scalar1=1e-12,
                                   scalar2=None, op0=ALU.max, op1=ALU.bypass), "V")
            pb.w(v, "V")
            pb.inc(v.reciprocal(out=cols[:, 431:432], in_=cols[:, 430:431]), "V")
            pb.w(v, "V")
            pb.inc(v.tensor_scalar(out=cols[:, 432:440], in0=cols[:, 422:430],
                                   scalar1=cols[:, 431:432], scalar2=None, op0=ALU.mult, op1=ALU.bypass), "V")
            pb.w(v, "V")
            for k in range(8):
                if k == 0:
                    pb.inc(v.tensor_scalar(out=comb[:, ti * 33: ti * 33 + 32], in0=iota32[:],
                                           scalar1=cols[:, 348:349], scalar2=cols[:, 432:433],
                                           op0=ALU.is_equal, op1=ALU.mult), "V")
                else:
                    pb.inc(v.tensor_scalar(out=cols[:, 440:472], in0=iota32[:],
                                           scalar1=cols[:, 348 + k:349 + k], scalar2=cols[:, 432 + k:433 + k],
                                           op0=ALU.is_equal, op1=ALU.mult), "V")
                    pb.w(v, "V")
                    pb.inc(v.tensor_tensor(out=comb[:, ti * 33: ti * 33 + 32],
                                           in0=comb[:, ti * 33: ti * 33 + 32],
                                           in1=cols[:, 440:472], op=ALU.add), "V")
                    pb.w(v, "V")

    for e in range(33):
        def gup_src(which, ic, e=e):
            if e < 32:
                t = io["L3_ffn_expert_gate"] if which == "g" else io["L3_ffn_expert_up"]
                return t[e, :, ic * 128:(ic + 1) * 128]
            t = io["L3_ffn_shared_gate_w"] if which == "g" else io["L3_ffn_shared_up_w"]
            return t[:, ic * 128:(ic + 1) * 128]

        if e == 0:
            gu_vg, gu_sg = {}, {}
            gu_i = [0]
        for ic in range(4):
            @block.sync
            def _(sync, e=e, ic=ic):
                pb.w(sync, "T")
                pb.dma(sync, bufC[:, 0:1024].rearrange("p (c f) -> p c f", f=128),
                       gup_src("g", ic).rearrange("(c p) f -> p c f", p=P), s="L")
                pb.dma(sync, bufC[:, 1024:2048].rearrange("p (c f) -> p c f", f=128),
                       gup_src("u", ic).rearrange("(c p) f -> p c f", p=P), s="L")
            L_w = pb.c["L"]

            for tg in range(4):
                i = gu_i[0]
                pp = i % 2
                b0, b1 = (0, 1) if pp == 0 else (6, 7)
                tmpv = tmp[:, 0:512] if pp == 0 else bufS[:, 1536:2048]

                @block.tensor
                def _(t, e=e, ic=ic, tg=tg, i=i, b0=b0, b1=b1):
                    pb.w(t, "L", L_w)
                    if i < 2:
                        pb.w(t, "V")
                        pb.w(t, "S")
                    else:
                        pb.w(t, "V", gu_vg[i - 2])
                        pb.w(t, "S", gu_sg[i - 2])
                    for c in range(NCH):
                        pb.inc(t.matmul(bank(b0), bufC[:, c * 128:(c + 1) * 128],
                                        lnT[:, c * 2048 + tg * 512: c * 2048 + (tg + 1) * 512],
                                        start=(c == 0), stop=(c == NCH - 1)), "T")
                    for c in range(NCH):
                        pb.inc(t.matmul(bank(b1), bufC[:, 1024 + c * 128: 1024 + (c + 1) * 128],
                                        lnT[:, c * 2048 + tg * 512: c * 2048 + (tg + 1) * 512],
                                        start=(c == 0), stop=(c == NCH - 1)), "T")

                @block.scalar
                def _(s, e=e, ic=ic, tg=tg, b0=b0, tmpv=tmpv):
                    pb.w(s, "T")
                    pb.inc(s.activation(out=tmpv, in_=bank(b0), func=AF.Silu), "S")
                gu_sg[i] = pb.c["S"]

                @block.vector
                def _(v, e=e, ic=ic, tg=tg, b1=b1, tmpv=tmpv):
                    pb.w(v, "S")
                    pb.inc(v.tensor_tensor(out=h_fm[:, ic * 2048 + tg * 512: ic * 2048 + (tg + 1) * 512],
                                           in0=bank(b1), in1=tmpv, op=ALU.mult), "V")
                gu_vg[i] = pb.c["V"]
                gu_i[0] = i + 1

        @block.sync
        def _(sync, e=e):
            pb.w(sync, "T")
            dsrc = io["L3_ffn_expert_down"][e] if e < 32 else io["L3_ffn_shared_down_w"]
            pb.dma(sync, bufC[:, 0:4096].rearrange("p (c f) -> p c f", f=1024),
                   dsrc.rearrange("(c p) f -> p c f", p=P), s="L")
        L_d = pb.c["L"]

        for ti in range(NT):
            pair = ti % 2

            @block.tensor
            def _(t, e=e, ti=ti, pair=pair):
                if ti == 0:
                    pb.w(t, "L", L_d)
                    pb.w(t, "V")
                elif ti >= 2:
                    pb.w(t, "V")
                for half in range(2):
                    for ic in range(4):
                        pb.inc(t.matmul(bank(2 + 2 * pair + half),
                                        h_fm[:, ic * 2048 + ti * 128: ic * 2048 + (ti + 1) * 128],
                                        bufC[:, ic * 1024 + half * 512: ic * 1024 + (half + 1) * 512],
                                        start=(ic == 0), stop=(ic == 3)), "T")

            @block.vector
            def _(v, e=e, ti=ti, pair=pair):
                pb.w(v, "T")
                xt = xr[:, ti * D:(ti + 1) * D]
                if e < 32:
                    pb.inc(v.scalar_tensor_tensor(out=xt, in0=PS[:, (2 + 2 * pair) * 512:(4 + 2 * pair) * 512],
                                                  scalar=comb[:, ti * 33 + e: ti * 33 + e + 1],
                                                  in1=xt, op0=ALU.mult, op1=ALU.add), "V")
                else:
                    pb.inc(v.tensor_tensor(out=xt, in0=PS[:, (2 + 2 * pair) * 512:(4 + 2 * pair) * 512],
                                           in1=xt, op=ALU.add), "V")


def emit_l2a(pb, io):
    block = pb.block
    xr = pb.sb["x_res"]
    bufA, bufB, bufC, bufS, cols = (pb.sb[k] for k in ["bufA", "bufB", "bufC", "bufS", "cols"])
    PS = pb.ps["PS"]
    ident = pb.sb["c_ident"]
    ones = pb.sb["c_ones"]
    bank = lambda i: PS[:, i * 512:(i + 1) * 512]

    Qfm = bufA
    Wsb = bufB
    lnTg = bufC
    lnt = bufS[:, 0:1024]
    evt = bufS[:, 1024:2048]
    mcols = cols[:, 264:328]
    Bb = cols[:, 200:264]

    kf_in, kf_out = io["d_kfin"], io["d_kfout"]
    v_in, v_out = io["d_vin"], io["d_vout"]

    def ln_group(g4, wait0):
        for ti2 in range(4):
            ti = g4 * 4 + ti2
            frag_ln(pb, xr[:, ti * D:(ti + 1) * D], lnt, wait_specs=wait0 if ti2 == 0 else ())
            frag_transpose_to(pb, lnt,
                              lambda c, ti2=ti2: lnTg[:, c * 512 + ti2 * 128: c * 512 + (ti2 + 1) * 128], NCH)

    @block.sync
    def _(sync):
        pb.w(sync, "T")
        pb.dma(sync, Wsb[:].rearrange("p (c f) -> p c f", f=D),
               io["L2_attn_Wq"].rearrange("(c p) f -> p c f", p=P), s="L")
    L_w = pb.c["L"]
    for g4 in range(4):
        ln_group(g4, (("L", L_w),))
        for oc in range(NCH):
            @block.tensor
            def _(t, g4=g4, oc=oc):
                pb.w(t, "V")
                for c in range(NCH):
                    pb.inc(t.matmul(bank(oc % 2), Wsb[:, c * D + oc * 128: c * D + (oc + 1) * 128],
                                    lnTg[:, c * 512:(c + 1) * 512],
                                    start=(c == 0), stop=(c == NCH - 1)), "T")

            @block.vector
            def _(v, g4=g4, oc=oc):
                pb.w(v, "T")
                pb.inc(v.tensor_copy(Qfm[:, oc * 2048 + g4 * 512: oc * 2048 + (g4 + 1) * 512],
                                     bank(oc % 2)), "V")

    @block.sync
    def _(sync):
        pb.w(sync, "T")
        pb.dma(sync, Wsb[:].rearrange("p (c f) -> p c f", f=D),
               io["L2_attn_Wk"].rearrange("(c p) f -> p c f", p=P), s="L")
    L_w = pb.c["L"]
    for g4 in range(4):
        ln_group(g4, (("L", L_w),))
        for oc in range(NCH):
            @block.tensor
            def _(t, g4=g4, oc=oc):
                pb.w(t, "V")
                pb.w(t, "ST")
                for c in range(NCH):
                    pb.inc(t.matmul(bank(oc % 2), Wsb[:, c * D + oc * 128: c * D + (oc + 1) * 128],
                                    lnTg[:, c * 512:(c + 1) * 512],
                                    start=(c == 0), stop=(c == NCH - 1)), "T")

            @block.vector
            def _(v, g4=g4, oc=oc):
                pb.w(v, "T")
                pb.w(v, "ST")
                pb.inc(v.tensor_copy(evt[:, 0:512], bank(oc % 2)), "V")

            @block.gpsimd
            def _(g, g4=g4, oc=oc):
                pb.w(g, "V")
                pb.dma(g, kf_in[oc * 128:(oc + 1) * 128, g4 * 512:(g4 + 1) * 512], evt[:, 0:512], s="ST")

            for h2 in range(2):
                for qt in range(4):
                    @block.tensor
                    def _(t, g4=g4, oc=oc, h2=h2, qt=qt):
                        pb.w(t, "V")
                        pb.inc(t.matmul(PS[0:128, 1024 + qt * 32: 1024 + (qt + 1) * 32],
                                        Qfm[h2 * 64:(h2 + 1) * 64,
                                            oc * 2048 + (g4 * 4 + qt) * 128: oc * 2048 + (g4 * 4 + qt + 1) * 128],
                                        evt[h2 * 64:(h2 + 1) * 64, qt * 128: qt * 128 + 32],
                                        start=True, stop=True), "T")

                @block.vector
                def _(v, g4=g4, oc=oc, h2=h2):
                    pb.w(v, "T")
                    for qt in range(4):
                        pb.inc(v.tensor_reduce(out=mcols[:, (oc * 2 + h2) * 4 + qt:(oc * 2 + h2) * 4 + qt + 1],
                                               in_=PS[0:128, 1024 + qt * 32: 1024 + (qt + 1) * 32],
                                               axis=AX.X, op=ALU.max), "V")

        @block.tensor
        def _(t, g4=g4):
            pb.w(t, "V")
            pb.inc(t.transpose(PS[0:64, 1024:1152], mcols[:, 0:64], ident[:]), "T")

        @block.vector
        def _(v, g4=g4):
            pb.w(v, "T")
            pb.inc(v.tensor_reduce(out=cols[0:64, 436:437], in_=PS[0:64, 1024:1152], axis=AX.X, op=ALU.max), "V")

        @block.tensor
        def _(t, g4=g4):
            pb.w(t, "V")
            pb.inc(t.transpose(PS[0:1, 1024:1088], cols[0:64, 436:437], ident[0:64, 0:64]), "T")

        @block.vector
        def _(v, g4=g4):
            pb.w(v, "T")
            pb.inc(v.tensor_reduce(out=cols[0:1, 437:453], in_=PS[0:1, 1024:1088].rearrange(
                "o (h q) -> o h q", q=4), axis=AX.X, op=ALU.max), "V")

        @block.tensor
        def _(t, g4=g4):
            pb.w(t, "V")
            pb.inc(t.matmul(PS[0:128, 1024:1040], ones[0:1, 0:128], cols[0:1, 437:453],
                            start=True, stop=True), "T")

        @block.vector
        def _(v, g4=g4):
            pb.w(v, "T")
            pb.inc(v.tensor_scalar(out=Bb[:, g4 * 16:(g4 + 1) * 16], in0=PS[0:128, 1024:1040],
                                   scalar1=-0.125, scalar2=None, op0=ALU.mult, op1=ALU.bypass), "V")

    @block.sync
    def _(sync):
        pb.w(sync, "T")
        pb.dma(sync, Wsb[:].rearrange("p (c f) -> p c f", f=D),
               io["L2_attn_Wv"].rearrange("(c p) f -> p c f", p=P), s="L")
    L_w = pb.c["L"]
    for g4 in range(4):
        ln_group(g4, (("L", L_w),))
        for ti2 in range(4):
            ti = g4 * 4 + ti2

            @block.tensor
            def _(t, g4=g4, ti2=ti2):
                pb.w(t, "V")
                pb.w(t, "ST")
                for half in range(2):
                    for c in range(NCH):
                        pb.inc(t.matmul(bank(half), lnTg[:, c * 512 + ti2 * 128: c * 512 + (ti2 + 1) * 128],
                                        Wsb[:, c * D + half * 512: c * D + (half + 1) * 512],
                                        start=(c == 0), stop=(c == NCH - 1)), "T")

            @block.vector
            def _(v, g4=g4, ti2=ti2):
                pb.w(v, "T")
                pb.w(v, "ST")
                pb.inc(v.tensor_copy(bufS[:, 0:1024], PS[:, 0:1024]), "V")

            @block.gpsimd
            def _(g, ti=ti):
                pb.w(g, "V")
                pb.dma(g, v_in[ti * P:(ti + 1) * P, :], bufS[:, 0:1024], s="ST")

    @block.gpsimd
    def _(g):
        pb.w(g, "ST")
        pb.inc(g.collective_compute("AllGather", ALU.bypass, replica_groups=GROUPS,
                                    ins=[kf_in], outs=[kf_out]), "CC")
        pb.inc(g.collective_compute("AllGather", ALU.bypass, replica_groups=GROUPS,
                                    ins=[v_in], outs=[v_out]), "CC")
    CC_kv = pb.c["CC"]

    @block.gpsimd
    def _(g):
        pb.w(g, "V")
        for slot in range(2):
            for sub in range(4):
                g.memset(bufC[:, 2048 + slot * 520 + sub * 130 + 64: 2048 + slot * 520 + sub * 130 + 65], 1.0)
                g.memset(bufC[:, 2048 + slot * 520 + sub * 130 + 129: 2048 + slot * 520 + sub * 130 + 130], 1.0)
        g.sem_inc(pb.sems["G"], 1)
    pb.c["G"] += 1
    G_ones = pb.c["G"]

    NSUP = 16
    for g4 in range(4):
        for hp in range(NCH):
            ksb = lambda s: bufC[:, s * 512:(s + 1) * 512]
            vsb = lambda s: bufC[:, 2048 + s * 520: 2048 + (s + 1) * 520]
            opair = bufC[:, 1024:1536]
            dn = bufC[0:1, 1536:2048]

            t_guard = []

            for ks in range(NSUP):
                @block.sync
                def _(sync, g4=g4, hp=hp, ks=ks):
                    if g4 == 0 and hp == 0 and ks == 0:
                        pb.w(sync, "CC", CC_kv)
                        pb.w(sync, "G", G_ones)
                        pb.w(sync, "T")
                    if ks >= 2:
                        pb.w(sync, "T", t_guard[ks - 2])
                    elif ks < 2 and (g4 or hp):
                        pb.w(sync, "T", t_guard_prev[NSUP - 2 + ks])
                    rk, kc = ks // 4, ks % 4
                    pb.dma(sync, ksb(ks % 2),
                           kf_out[rk * 1024 + hp * 128: rk * 1024 + (hp + 1) * 128,
                                  kc * 512:(kc + 1) * 512], s="L")
                    for h2 in range(2):
                        pb.dma(sync, bass.AP(bufC.ap().tensor, 2048 + (ks % 2) * 520 + h2 * 65,
                                             [[4096, 128], [130, 4], [1, 64]]),
                               v_out[ks * 512:(ks + 1) * 512, hp * 128 + h2 * 64: hp * 128 + (h2 + 1) * 64]
                               .rearrange("(s p) f -> p s f", p=P), s="L")
                L_k = pb.c["L"]

                for hsl in range(2):
                    s_exp = {}

                    @block.tensor
                    def _(t, g4=g4, hp=hp, ks=ks, hsl=hsl):
                        if hsl == 0:
                            pb.w(t, "L", L_k)
                            if g4 == 0 and hp == 0 and ks == 0:
                                pb.w(t, "V")
                        pb.w(t, "S")
                        for sp in range(2):
                            sub = hsl * 2 + sp
                            pb.inc(t.matmul(bank(2 + sp), ksb(ks % 2)[0:64, sub * 128:(sub + 1) * 128],
                                            Qfm[0:64, hp * 2048 + g4 * 512: hp * 2048 + (g4 + 1) * 512],
                                            start=True, stop=True), "T")

                    @block.scalar
                    def _(s, g4=g4, hp=hp, ks=ks, hsl=hsl):
                        pb.w(s, "T")
                        pb.inc(s.activation(out=bufS[:, 0:1024],
                                            in_=PS[:, 1024:2048],
                                            func=AF.Exp, scale=0.125,
                                            bias=Bb[:, g4 * 16 + hp * 2: g4 * 16 + hp * 2 + 1]), "S")
                    s_exp[0] = pb.c["S"]

                    @block.tensor
                    def _(t, g4=g4, hp=hp, ks=ks, hsl=hsl):
                        for sp in range(2):
                            sub = hsl * 2 + sp
                            pb.inc(t.matmul(bank(4 + sp), ksb(ks % 2)[64:128, sub * 128:(sub + 1) * 128],
                                            Qfm[64:128, hp * 2048 + g4 * 512: hp * 2048 + (g4 + 1) * 512],
                                            start=True, stop=True, tile_position=(64, 0)), "T")

                    @block.scalar
                    def _(s, g4=g4, hp=hp, ks=ks, hsl=hsl):
                        pb.w(s, "T")
                        pb.inc(s.activation(out=bufS[:, 1024:2048],
                                            in_=PS[:, 2048:3072],
                                            func=AF.Exp, scale=0.125,
                                            bias=Bb[:, g4 * 16 + hp * 2 + 1: g4 * 16 + hp * 2 + 2]), "S")
                    s_exp[1] = pb.c["S"]

                    @block.tensor
                    def _(t, g4=g4, hp=hp, ks=ks, hsl=hsl, s_exp=s_exp):
                        first = (ks == 0 and hsl == 0)
                        last = (ks == NSUP - 1 and hsl == 1)
                        for h2 in range(2):
                            pb.w(t, "S", s_exp[h2])
                            for sp in range(2):
                                sub = hsl * 2 + sp
                                pb.inc(t.matmul(PS[0:65, h2 * 512:(h2 + 1) * 512],
                                                vsb(ks % 2)[:, sub * 130 + h2 * 65: sub * 130 + (h2 + 1) * 65],
                                                bufS[:, h2 * 1024 + sp * 512: h2 * 1024 + (sp + 1) * 512],
                                                start=(first and sp == 0), stop=(last and sp == 1)), "T")
                t_guard.append(pb.c["T"])
            t_guard_prev = t_guard

            for h2 in range(2):
                @block.vector
                def _(v, g4=g4, hp=hp, h2=h2):
                    pb.w(v, "T")
                    pb.inc(v.tensor_copy(dn[0:1, 0:512], PS[64:65, h2 * 512:(h2 + 1) * 512]), "V")
                    pb.w(v, "V")
                    pb.inc(v.reciprocal(out=dn[0:1, 0:512], in_=dn[0:1, 0:512]), "V")

                @block.tensor
                def _(t, g4=g4, hp=hp, h2=h2):
                    pb.w(t, "V")
                    pb.inc(t.matmul(PS[0:64, 3072:3584], ones[0:1, 0:64], dn[0:1, 0:512],
                                    start=True, stop=True), "T")

                @block.vector
                def _(v, g4=g4, hp=hp, h2=h2):
                    pb.w(v, "T")
                    pb.inc(v.tensor_copy(bufS[0:64, 0:512], PS[0:64, 3072:3584]), "V")
                    pb.w(v, "V")
                    pb.inc(v.tensor_tensor(out=opair[h2 * 64:(h2 + 1) * 64, 0:512],
                                           in0=PS[0:64, h2 * 512:(h2 + 1) * 512],
                                           in1=bufS[0:64, 0:512], op=ALU.mult), "V")

            @block.tensor
            def _(t, g4=g4, hp=hp):
                pb.w(t, "V")
                for qs in range(4):
                    pb.inc(t.transpose(PS[:, 3072 + qs * 128: 3072 + (qs + 1) * 128],
                                       opair[:, qs * 128:(qs + 1) * 128], ident[:]), "T")

            @block.vector
            def _(v, g4=g4, hp=hp):
                pb.w(v, "T")
                for qs in range(4):
                    ti = g4 * 4 + qs
                    xt = xr[:, ti * D + hp * 128: ti * D + (hp + 1) * 128]
                    pb.inc(v.tensor_tensor(out=xt, in0=PS[:, 3072 + qs * 128: 3072 + (qs + 1) * 128],
                                           in1=xt, op=ALU.add), "V")


def emit_l3a(pb, io):
    block = pb.block
    xr = pb.sb["x_res"]
    bufA, bufB, bufC, bufS, cols = (pb.sb[k] for k in ["bufA", "bufB", "bufC", "bufS", "cols"])
    PS = pb.ps["PS"]
    ident = pb.sb["c_ident"]
    ones = pb.sb["c_ones"]
    cwin = pb.sb["c_win"]
    predm = pb.sb["c_predmask"]
    bank = lambda i: PS[:, i * 512:(i + 1) * 512]

    Qfm = bufA
    Wsb = bufB
    lnTg = bufC
    lnt = bufS[:, 0:1024]
    evt = bufS[:, 1024:2048]

    d_kf, d_v = io["d_kf"], io["d_v"]
    d_ktin, d_ktout = io["d_ktin"], io["d_ktout"]
    d_vtin, d_vtout = io["d_vtin"], io["d_vtout"]
    d_att = io["d_att"]

    def ln_group(g4, wait0):
        for ti2 in range(4):
            ti = g4 * 4 + ti2
            frag_ln(pb, xr[:, ti * D:(ti + 1) * D], lnt, wait_specs=wait0 if ti2 == 0 else ())
            frag_transpose_to(pb, lnt,
                              lambda c, ti2=ti2: lnTg[:, c * 512 + ti2 * 128: c * 512 + (ti2 + 1) * 128], NCH)

    @block.sync
    def _(sync):
        pb.w(sync, "T")
        pb.dma(sync, Wsb[:].rearrange("p (c f) -> p c f", f=D),
               io["L3_attn_Wq"].rearrange("(c p) f -> p c f", p=P), s="L")
    L_w = pb.c["L"]
    for g4 in range(4):
        ln_group(g4, (("L", L_w),))
        for oc in range(NCH):
            @block.tensor
            def _(t, g4=g4, oc=oc):
                pb.w(t, "V")
                for c in range(NCH):
                    pb.inc(t.matmul(bank(oc % 2), Wsb[:, c * D + oc * 128: c * D + (oc + 1) * 128],
                                    lnTg[:, c * 512:(c + 1) * 512],
                                    start=(c == 0), stop=(c == NCH - 1)), "T")

            @block.vector
            def _(v, g4=g4, oc=oc):
                pb.w(v, "T")
                pb.inc(v.tensor_copy(Qfm[:, oc * 2048 + g4 * 512: oc * 2048 + (g4 + 1) * 512],
                                     bank(oc % 2)), "V")

    @block.sync
    def _(sync):
        pb.w(sync, "T")
        pb.dma(sync, Wsb[:].rearrange("p (c f) -> p c f", f=D),
               io["L3_attn_Wk"].rearrange("(c p) f -> p c f", p=P), s="L")
    L_w = pb.c["L"]
    for g4 in range(4):
        ln_group(g4, (("L", L_w),))
        for oc in range(NCH):
            @block.tensor
            def _(t, g4=g4, oc=oc):
                pb.w(t, "V")
                pb.w(t, "ST")
                for c in range(NCH):
                    pb.inc(t.matmul(bank(oc % 2), Wsb[:, c * D + oc * 128: c * D + (oc + 1) * 128],
                                    lnTg[:, c * 512:(c + 1) * 512],
                                    start=(c == 0), stop=(c == NCH - 1)), "T")

            @block.vector
            def _(v, g4=g4, oc=oc):
                pb.w(v, "T")
                pb.w(v, "ST")
                pb.inc(v.tensor_copy(evt[:, 0:512], bank(oc % 2)), "V")

            @block.gpsimd
            def _(g, g4=g4, oc=oc):
                pb.w(g, "V")
                pb.dma(g, d_kf[oc * 128:(oc + 1) * 128, g4 * 512:(g4 + 1) * 512], evt[:, 0:512], s="ST")
                if g4 == 3:
                    pb.dma(g, d_ktin[oc * 128:(oc + 1) * 128, :], evt[:, 384:512], s="ST")

    @block.sync
    def _(sync):
        pb.w(sync, "T")
        pb.dma(sync, Wsb[:].rearrange("p (c f) -> p c f", f=D),
               io["L3_attn_Wv"].rearrange("(c p) f -> p c f", p=P), s="L")
    L_w = pb.c["L"]
    for g4 in range(4):
        ln_group(g4, (("L", L_w),))
        for ti2 in range(4):
            ti = g4 * 4 + ti2

            @block.tensor
            def _(t, g4=g4, ti2=ti2):
                pb.w(t, "V")
                pb.w(t, "ST")
                for half in range(2):
                    for c in range(NCH):
                        pb.inc(t.matmul(bank(half), lnTg[:, c * 512 + ti2 * 128: c * 512 + (ti2 + 1) * 128],
                                        Wsb[:, c * D + half * 512: c * D + (half + 1) * 512],
                                        start=(c == 0), stop=(c == NCH - 1)), "T")

            @block.vector
            def _(v, g4=g4, ti2=ti2):
                pb.w(v, "T")
                pb.w(v, "ST")
                pb.inc(v.tensor_copy(bufS[:, 0:1024], PS[:, 0:1024]), "V")

            @block.gpsimd
            def _(g, ti=ti):
                pb.w(g, "V")
                pb.dma(g, d_v[ti * P:(ti + 1) * P, :], bufS[:, 0:1024], s="ST")
                if ti == 15:
                    pb.dma(g, d_vtin[:, :], bufS[:, 0:1024], s="ST")

    @block.gpsimd
    def _(g):
        pb.w(g, "ST")
        pb.inc(g.collective_compute("AllGather", ALU.bypass, replica_groups=GROUPS,
                                    ins=[d_ktin], outs=[d_ktout]), "CC")
        pb.inc(g.collective_compute("AllGather", ALU.bypass, replica_groups=GROUPS,
                                    ins=[d_vtin], outs=[d_vtout]), "CC")
    for r in range(4):
        @block.sync
        def _(sync, r=r):
            if r == 0:
                pb.w(sync, "CC")
            pb.w(sync, "V")
            pb.dma(sync, bufS[:, 0:1024].rearrange("p (c f) -> p c f", f=128),
                   d_ktout[r * 1024:(r + 1) * 1024, :].rearrange("(c p) f -> p c f", p=P), s="L")
        L_r = pb.c["L"]

        @block.vector
        def _(v, r=r):
            pb.w(v, "L", L_r)
            if r == 0:
                pb.inc(v.tensor_scalar(out=bufC[:, 0:1024], in0=bufS[:, 0:1024],
                                       scalar1=predm[:, r:r + 1], scalar2=None,
                                       op0=ALU.mult, op1=ALU.bypass), "V")
            else:
                pb.w(v, "V")
                pb.inc(v.scalar_tensor_tensor(out=bufC[:, 0:1024], in0=bufS[:, 0:1024],
                                              scalar=predm[:, r:r + 1], in1=bufC[:, 0:1024],
                                              op0=ALU.mult, op1=ALU.add), "V")
    for r in range(4):
        @block.sync
        def _(sync, r=r):
            pb.w(sync, "V")
            pb.dma(sync, bufS[:, 0:1024], d_vtout[r * 128:(r + 1) * 128, :], s="L")
        L_r = pb.c["L"]

        @block.vector
        def _(v, r=r):
            pb.w(v, "L", L_r)
            dstv = bufC[:, 1024:2064].rearrange("p (h x) -> p h x", x=65)[:, :, 0:64]
            srcv = bufS[:, 0:1024].rearrange("p (h x) -> p h x", x=64)
            if r == 0:
                pb.inc(v.tensor_scalar(out=dstv, in0=srcv, scalar1=predm[:, r:r + 1],
                                       scalar2=None, op0=ALU.mult, op1=ALU.bypass), "V")
            else:
                pb.w(v, "V")
                pb.inc(v.scalar_tensor_tensor(out=dstv, in0=srcv, scalar=predm[:, r:r + 1],
                                              in1=bufC[:, 1024:2064].rearrange("p (h x) -> p h x", x=65)[:, :, 0:64],
                                              op0=ALU.mult, op1=ALU.add), "V")

    @block.vector
    def _(v):
        pb.w(v, "V")
        pb.inc(v.tensor_reduce(out=cols[:, 59:60], in_=predm[:], axis=AX.X, op=ALU.add), "V")
        pb.w(v, "V")
        pb.inc(v.tensor_scalar(out=cols[:, 60:61], in0=cols[:, 59:60], scalar1=1.0,
                               scalar2=1.0e30, op0=ALU.subtract, op1=ALU.mult), "V")

    @block.gpsimd
    def _(g):
        pb.w(g, "V")
        for h in range(16):
            g.memset(bufC[:, 1024 + h * 65 + 64: 1024 + h * 65 + 65], 1.0)
        for slot in range(2):
            for h2 in range(2):
                g.memset(bufC[:, 2064 + slot * 130 + h2 * 65 + 64: 2064 + slot * 130 + h2 * 65 + 65], 1.0)
        g.sem_inc(pb.sems["G"], 1)
    pb.c["G"] += 1
    G_ones = pb.c["G"]

    ksb = lambda s: bufC[:, 2324 + s * 128: 2324 + (s + 1) * 128]
    vsb = lambda s: bufC[:, 2064 + s * 130: 2064 + (s + 1) * 130]
    opair = bufC[:, 2580 + 0:2580 + 512]
    dn = bufC[0:1, 3092:3604]
    WoT_loaded = [False]

    for g4 in range(4):
        for hp in range(NCH):
            kts = list(range(4 * g4 - 1, 4 * g4 + 4))
            for ki, kt in enumerate(kts):
                slot = ki % 2
                if kt >= 0:
                    @block.sync
                    def _(sync, g4=g4, hp=hp, kt=kt, ki=ki, slot=slot):
                        if g4 == 0 and hp == 0 and ki <= 1:
                            pb.w(sync, "G", G_ones)
                            pb.w(sync, "T")
                        else:
                            pb.w(sync, "T")
                        pb.dma(sync, ksb(slot), d_kf[hp * 128:(hp + 1) * 128, kt * 128:(kt + 1) * 128], s="L")
                        pb.dma(sync, bass.AP(bufC.ap().tensor, 2064 + slot * 130,
                                             [[4096, 128], [65, 2], [1, 64]]),
                               d_v[kt * 128:(kt + 1) * 128, hp * 128:(hp + 1) * 128]
                               .rearrange("p (h f) -> p h f", f=64), s="L")
                L_k = pb.c["L"]

                @block.tensor
                def _(t, g4=g4, hp=hp, kt=kt, ki=ki, slot=slot):
                    pb.w(t, "L", L_k)
                    pb.w(t, "V")
                    ksrc = ksb(slot) if kt >= 0 else bufC[:, hp * 128:(hp + 1) * 128]
                    for h2 in range(2):
                        pb.inc(t.matmul(bank(2 + h2), ksrc[h2 * 64:(h2 + 1) * 64, :],
                                        Qfm[h2 * 64:(h2 + 1) * 64, hp * 2048 + g4 * 512: hp * 2048 + (g4 + 1) * 512],
                                        start=True, stop=True,
                                        tile_position=(h2 * 64, 0)), "T")

                o = kt * 128 - g4 * 512
                msk = cwin[:, 384 - o: 384 - o + 512]

                @block.vector
                def _(v, g4=g4, hp=hp, kt=kt, ki=ki):
                    pb.w(v, "T")
                    for h2 in range(2):
                        pb.inc(v.scalar_tensor_tensor(out=bufS[:, h2 * 512:(h2 + 1) * 512],
                                                      in0=PS[:, (2 + h2) * 512:(3 + h2) * 512],
                                                      scalar=0.125, in1=msk,
                                                      op0=ALU.mult, op1=ALU.add), "V")
                    if kt < 0:
                        pb.w(v, "V")
                        for h2 in range(2):
                            pb.inc(v.tensor_scalar(out=bufS[:, h2 * 512:(h2 + 1) * 512],
                                                   in0=bufS[:, h2 * 512:(h2 + 1) * 512],
                                                   scalar1=cols[:, 60:61], scalar2=None,
                                                   op0=ALU.add, op1=ALU.bypass), "V")

                @block.scalar
                def _(s, g4=g4, hp=hp, kt=kt, ki=ki):
                    pb.w(s, "V")
                    pb.w(s, "T")
                    for h2 in range(2):
                        pb.inc(s.activation(out=bufS[:, 1024 + h2 * 512: 1024 + (h2 + 1) * 512],
                                            in_=bufS[:, h2 * 512:(h2 + 1) * 512], func=AF.Exp), "S")

                @block.tensor
                def _(t, g4=g4, hp=hp, kt=kt, ki=ki, slot=slot):
                    pb.w(t, "S")
                    vsrc = vsb(slot) if kt >= 0 else None
                    for h2 in range(2):
                        if kt >= 0:
                            lhs = vsrc[:, h2 * 65:(h2 + 1) * 65]
                        else:
                            lhs = bufC[:, 1024 + (hp * 2 + h2) * 65: 1024 + (hp * 2 + h2 + 1) * 65]
                        pb.inc(t.matmul(PS[0:65, h2 * 512:(h2 + 1) * 512], lhs,
                                        bufS[:, 1024 + h2 * 512: 1024 + (h2 + 1) * 512],
                                        start=(ki == 0), stop=(ki == len(kts) - 1)), "T")

            for h2 in range(2):
                @block.vector
                def _(v, g4=g4, hp=hp, h2=h2):
                    pb.w(v, "T")
                    pb.inc(v.tensor_copy(dn[0:1, 0:512], PS[64:65, h2 * 512:(h2 + 1) * 512]), "V")
                    pb.w(v, "V")
                    pb.inc(v.reciprocal(out=dn[0:1, 0:512], in_=dn[0:1, 0:512]), "V")

                @block.tensor
                def _(t, g4=g4, hp=hp, h2=h2):
                    pb.w(t, "V")
                    pb.inc(t.matmul(PS[0:64, 3072:3584], ones[0:1, 0:64], dn[0:1, 0:512],
                                    start=True, stop=True), "T")

                @block.vector
                def _(v, g4=g4, hp=hp, h2=h2):
                    pb.w(v, "T")
                    pb.w(v, "ST")
                    pb.inc(v.tensor_copy(bufS[0:64, 0:512], PS[0:64, 3072:3584]), "V")
                    pb.w(v, "V")
                    pb.inc(v.tensor_tensor(out=opair[h2 * 64:(h2 + 1) * 64, 0:512],
                                           in0=PS[0:64, h2 * 512:(h2 + 1) * 512],
                                           in1=bufS[0:64, 0:512], op=ALU.mult), "V")

            @block.gpsimd
            def _(g, g4=g4, hp=hp):
                pb.w(g, "V")
                pb.dma(g, d_att[hp * 128:(hp + 1) * 128, g4 * 512:(g4 + 1) * 512], opair[:, 0:512], s="ST")

    @block.sync
    def _(sync):
        pb.w(sync, "T")
        pb.dma(sync, Wsb[:].rearrange("p (c f) -> p c f", f=D),
               io["L3_attn_Wo"].rearrange("(c p) f -> p c f", p=P), s="L")
    L_wo = pb.c["L"]
    for g4 in range(4):
        @block.sync
        def _(sync, g4=g4):
            pb.w(sync, "ST")
            if g4 > 0:
                pb.w(sync, "T")
            pb.dma(sync, bufC[:, 0:4096].rearrange("p (c f) -> p c f", f=512),
                   d_att[:, g4 * 512:(g4 + 1) * 512].rearrange("(c p) f -> p c f", p=P), s="L")
        L_att = pb.c["L"]

        for qs in range(4):
            ti = g4 * 4 + qs

            @block.tensor
            def _(t, g4=g4, qs=qs, ti=ti):
                pb.w(t, "L", L_att)
                if g4 == 0 and qs == 0:
                    pb.w(t, "L", L_wo)
                pb.w(t, "V")
                for half in range(2):
                    for c8 in range(NCH):
                        pb.inc(t.matmul(bank(4 + half),
                                        bufC[:, c8 * 512 + qs * 128: c8 * 512 + (qs + 1) * 128],
                                        Wsb[:, c8 * D + half * 512: c8 * D + (half + 1) * 512],
                                        start=(c8 == 0), stop=(c8 == NCH - 1)), "T")

            @block.vector
            def _(v, g4=g4, qs=qs, ti=ti):
                pb.w(v, "T")
                xt = xr[:, ti * D:(ti + 1) * D]
                pb.inc(v.tensor_tensor(out=xt, in0=PS[:, 2048:3072], in1=xt, op=ALU.add), "V")


def emit_l1a(pb, io):
    block = pb.block
    xr = pb.sb["x_res"]
    bufA, bufB, bufC, bufS, cols = (pb.sb[k] for k in ["bufA", "bufB", "bufC", "bufS", "cols"])
    gcols = pb.sb["gcols"]
    PS = pb.ps["PS"]
    ident = pb.sb["c_ident"]
    ones = pb.sb["c_ones"]
    ccau = pb.sb["c_causal"]
    predm = pb.sb["c_predmask"]
    bank = lambda i: PS[:, i * 512:(i + 1) * 512]

    Qfm = bufA
    Wsb = bufB
    lnTg = bufC
    lnt = bufS[:, 0:1024]
    evt = bufS[:, 1024:2048]

    d_kf, d_v, d_lgt, d_ct, d_wg = (io[k] for k in ["d_kf", "d_v", "d_lgt", "d_ct", "d_wg"])
    d_kfo, d_vo, d_cto, d_wgo = (io[k] for k in ["d_kfo", "d_vo", "d_cto", "d_wgo"])
    d_kp, d_vp, d_att = io["d_kp"], io["d_vp"], io["d_att"]

    def ln_group(g4, wait0):
        for ti2 in range(4):
            ti = g4 * 4 + ti2
            frag_ln(pb, xr[:, ti * D:(ti + 1) * D], lnt, wait_specs=wait0 if ti2 == 0 else ())
            frag_transpose_to(pb, lnt,
                              lambda c, ti2=ti2: lnTg[:, c * 512 + ti2 * 128: c * 512 + (ti2 + 1) * 128], NCH)

    def phi_evac(v, dst, src_ps):
        pb.w(v, "S")
        pb.inc(v.scalar_tensor_tensor(out=bufS[:, 0:512],
                                      in0=src_ps, scalar=1.0, in1=evt[:, 0:512],
                                      op0=ALU.add, op1=ALU.subtract), "V")
        pb.inc(v.tensor_scalar(out=bufS[:, 512:1024], in0=src_ps, scalar1=0.0, scalar2=None,
                               op0=ALU.is_gt, op1=ALU.bypass), "V")
        pb.w(v, "V")
        pb.inc(v.tensor_tensor(out=bufS[:, 0:512], in0=bufS[:, 0:512],
                               in1=bufS[:, 512:1024], op=ALU.mult), "V")
        pb.w(v, "V")
        pb.inc(v.tensor_tensor(out=dst, in0=bufS[:, 0:512], in1=evt[:, 0:512], op=ALU.add), "V")

    @block.sync
    def _(sync):
        pb.w(sync, "T")
        pb.dma(sync, Wsb[:].rearrange("p (c f) -> p c f", f=D),
               io["L1_attn_Wq"].rearrange("(c p) f -> p c f", p=P), s="L")
    L_w = pb.c["L"]
    for g4 in range(4):
        ln_group(g4, (("L", L_w),))
        for oc in range(NCH):
            @block.tensor
            def _(t, g4=g4, oc=oc):
                pb.w(t, "V")
                for c in range(NCH):
                    pb.inc(t.matmul(bank(oc % 2), Wsb[:, c * D + oc * 128: c * D + (oc + 1) * 128],
                                    lnTg[:, c * 512:(c + 1) * 512],
                                    start=(c == 0), stop=(c == NCH - 1)), "T")

            @block.scalar
            def _(s, g4=g4, oc=oc):
                pb.w(s, "T")
                pb.w(s, "V")
                pb.inc(s.activation(out=evt[:, 0:512], in_=bank(oc % 2), func=AF.Exp), "S")

            @block.vector
            def _(v, g4=g4, oc=oc):
                phi_evac(v, Qfm[:, oc * 2048 + g4 * 512: oc * 2048 + (g4 + 1) * 512], bank(oc % 2))

    @block.sync
    def _(sync):
        pb.w(sync, "T")
        pb.dma(sync, Wsb[:].rearrange("p (c f) -> p c f", f=D),
               io["L1_attn_Wk"].rearrange("(c p) f -> p c f", p=P), s="L")
    L_w = pb.c["L"]
    kevt = bufS[:, 0:512]
    for g4 in range(4):
        ln_group(g4, (("L", L_w),))
        for oc in range(NCH):
            @block.tensor
            def _(t, g4=g4, oc=oc):
                pb.w(t, "V")
                pb.w(t, "ST")
                for c in range(NCH):
                    pb.inc(t.matmul(bank(oc % 2), Wsb[:, c * D + oc * 128: c * D + (oc + 1) * 128],
                                    lnTg[:, c * 512:(c + 1) * 512],
                                    start=(c == 0), stop=(c == NCH - 1)), "T")

            @block.scalar
            def _(s, g4=g4, oc=oc):
                pb.w(s, "T")
                pb.w(s, "V")
                pb.inc(s.activation(out=evt[:, 0:512], in_=bank(oc % 2), func=AF.Exp), "S")

            @block.vector
            def _(v, g4=g4, oc=oc):
                pb.w(v, "ST")
                phi_evac(v, evt[:, 512:1024], bank(oc % 2))

            @block.gpsimd
            def _(g, g4=g4, oc=oc):
                pb.w(g, "V")
                pb.dma(g, d_kf[oc * 128:(oc + 1) * 128, g4 * 512:(g4 + 1) * 512], evt[:, 512:1024], s="ST")

    @block.sync
    def _(sync):
        pb.w(sync, "T")
        pb.dma(sync, Wsb[:].rearrange("p (c f) -> p c f", f=D),
               io["L1_attn_Wv"].rearrange("(c p) f -> p c f", p=P), s="L")
        pb.dma(sync, gcols[0:128, 0:128].rearrange("p (c f) -> p c f", f=16),
               io["L1_attn_Ww"].rearrange("(c p) f -> p c f", p=P), s="L")
        pb.dma(sync, gcols[0:128, 128:256].rearrange("p (c f) -> p c f", f=16),
               io["L1_attn_We"].rearrange("(c p) f -> p c f", p=P), s="L")
        pb.dma(sync, cols[0:1, 32:48], io["L1_attn_bw"].unsqueeze(0), s="L")
        pb.dma(sync, cols[0:1, 48:64], io["L1_attn_be"].unsqueeze(0), s="L")
    L_w = pb.c["L"]
    for g4 in range(4):
        ln_group(g4, (("L", L_w),))
        for ti2 in range(4):
            ti = g4 * 4 + ti2

            @block.tensor
            def _(t, g4=g4, ti2=ti2):
                pb.w(t, "V")
                pb.w(t, "ST")
                for half in range(2):
                    for c in range(NCH):
                        pb.inc(t.matmul(bank(half), lnTg[:, c * 512 + ti2 * 128: c * 512 + (ti2 + 1) * 128],
                                        Wsb[:, c * D + half * 512: c * D + (half + 1) * 512],
                                        start=(c == 0), stop=(c == NCH - 1)), "T")
                for c in range(NCH):
                    pb.inc(t.matmul(PS[:, 1024:1040], lnTg[:, c * 512 + ti2 * 128: c * 512 + (ti2 + 1) * 128],
                                    gcols[:, c * 16:(c + 1) * 16], start=(c == 0), stop=False), "T")
                pb.inc(t.matmul(PS[:, 1024:1040], ones[0:1, 0:128], cols[0:1, 32:48],
                                start=False, stop=True), "T")
                for c in range(NCH):
                    pb.inc(t.matmul(PS[:, 1040:1056], lnTg[:, c * 512 + ti2 * 128: c * 512 + (ti2 + 1) * 128],
                                    gcols[:, 128 + c * 16: 128 + (c + 1) * 16], start=(c == 0), stop=False), "T")
                pb.inc(t.matmul(PS[:, 1040:1056], ones[0:1, 0:128], cols[0:1, 48:64],
                                start=False, stop=True), "T")

            @block.scalar
            def _(s, g4=g4, ti=ti):
                pb.w(s, "T")
                pb.inc(s.activation(out=gcols[:, 256 + ti * 16: 256 + (ti + 1) * 16],
                                    in_=PS[:, 1024:1040], func=AF.Sigmoid), "S")
                pb.inc(s.activation(out=cols[:, 64:80], in_=PS[:, 1040:1056], func=AF.Sigmoid), "S")

            @block.vector
            def _(v, g4=g4, ti=ti):
                pb.w(v, "S")
                pb.w(v, "ST")
                pb.inc(v.tensor_copy(bufS[:, 0:1024], PS[:, 0:1024]), "V")
                pb.inc(v.tensor_scalar(out=cols[:, 80:96], in0=cols[:, 64:80],
                                       scalar1=1.0, scalar2=-0.95, op0=ALU.subtract, op1=ALU.mult), "V")

            @block.scalar
            def _(s, g4=g4, ti=ti):
                pb.w(s, "V")
                pb.inc(s.activation(out=cols[:, 96:112], in_=cols[:, 80:96], func=AF.Ln,
                                    bias=pb.sb["c_eps"][:, 0:1]), "S")

            @block.tensor
            def _(t, g4=g4, ti=ti):
                pb.w(t, "S")
                pb.inc(t.transpose(PS[0:16, 3072:3200], cols[:, 96:112], ident[:]), "T")

            @block.vector
            def _(v, g4=g4, ti=ti):
                pb.w(v, "T")
                pb.inc(v.tensor_copy(cols[0:16, 112:240], PS[0:16, 3072:3200]), "V")

            @block.gpsimd
            def _(g, g4=g4, ti=ti):
                pb.w(g, "V")
                pb.dma(g, d_v[ti * P:(ti + 1) * P, :], bufS[:, 0:1024], s="ST")
                pb.dma(g, d_lgt[:, ti * 128:(ti + 1) * 128], cols[0:16, 112:240], s="ST")
                pb.dma(g, d_wg[ti * P:(ti + 1) * P, :], gcols[:, 256 + ti * 16: 256 + (ti + 1) * 16], s="ST")

    @block.sync
    def _(sync):
        pb.w(sync, "ST")
        pb.dma(sync, bufS[0:16, 0:2048], d_lgt, s="L")
    L_lg = pb.c["L"]

    @block.vector
    def _(v):
        pb.w(v, "L", L_lg)
        pb.inc(v.tensor_tensor_scan(out=bufS[64:80, 0:2048], data0=bufS[0:16, 0:2048],
                                    data1=bufS[0:16, 0:2048], initial=0.0,
                                    op0=ALU.add, op1=ALU.bypass), "V")

    @block.gpsimd
    def _(g):
        pb.w(g, "V")
        pb.dma(g, d_ct, bufS[64:80, 0:2048], s="ST")

    for kt in range(16):
        @block.tensor
        def _(t, kt=kt):
            pb.w(t, "V")
            pb.inc(t.transpose(PS[0:128, 3072 + (kt % 4) * 128: 3072 + (kt % 4) * 128 + 16],
                               bufS[64:80, kt * 128:(kt + 1) * 128],
                               ident[64:80, 64:80]), "T")

        @block.vector
        def _(v, kt=kt):
            pb.w(v, "T")
            pb.inc(v.tensor_scalar(out=gcols[:, 512 + kt * 16: 512 + (kt + 1) * 16],
                                   in0=PS[0:128, 3072 + (kt % 4) * 128: 3072 + (kt % 4) * 128 + 16],
                                   scalar1=-1.0, scalar2=None, op0=ALU.mult, op1=ALU.bypass), "V")

    @block.gpsimd
    def _(g):
        pb.w(g, "ST")
        pb.w(g, "V")
        pb.inc(g.collective_compute("AllGather", ALU.bypass, replica_groups=GROUPS,
                                    ins=[d_kf], outs=[d_kfo]), "CC")
        pb.inc(g.collective_compute("AllGather", ALU.bypass, replica_groups=GROUPS,
                                    ins=[d_v], outs=[d_vo]), "CC")
        pb.inc(g.collective_compute("AllGather", ALU.bypass, replica_groups=GROUPS,
                                    ins=[d_ct], outs=[d_cto]), "CC")
        pb.inc(g.collective_compute("AllGather", ALU.bypass, replica_groups=GROUPS,
                                    ins=[d_wg], outs=[d_wgo]), "CC")
    CC_all = pb.c["CC"]

    for c8 in range(NCH):
        for r in range(4):
            @block.sync
            def _(sync, c8=c8, r=r):
                if c8 == 0 and r == 0:
                    pb.w(sync, "CC", CC_all)
                pb.w(sync, "V")
                pb.dma(sync, bufC[:, 2048:4096], d_kfo[r * 1024 + c8 * 128: r * 1024 + (c8 + 1) * 128, :], s="L")
            L_r = pb.c["L"]

            @block.vector
            def _(v, c8=c8, r=r):
                pb.w(v, "L", L_r)
                if r == 0:
                    pb.inc(v.tensor_scalar(out=bufC[:, 0:2048], in0=bufC[:, 2048:4096],
                                           scalar1=predm[:, r:r + 1], scalar2=None,
                                           op0=ALU.mult, op1=ALU.bypass), "V")
                else:
                    pb.w(v, "V")
                    pb.inc(v.scalar_tensor_tensor(out=bufC[:, 0:2048], in0=bufC[:, 2048:4096],
                                                  scalar=predm[:, r:r + 1], in1=bufC[:, 0:2048],
                                                  op0=ALU.mult, op1=ALU.add), "V")

        @block.gpsimd
        def _(g, c8=c8):
            pb.w(g, "V")
            pb.dma(g, d_kp[c8 * 128:(c8 + 1) * 128, :], bufC[:, 0:2048], s="ST")

    for ti in range(NT):
        for r in range(4):
            @block.sync
            def _(sync, ti=ti, r=r):
                pb.w(sync, "V")
                pb.w(sync, "ST")
                pb.dma(sync, bufC[:, 2048:3072], d_vo[r * 2048 + ti * 128: r * 2048 + (ti + 1) * 128, :], s="L")
            L_r = pb.c["L"]

            @block.vector
            def _(v, ti=ti, r=r):
                pb.w(v, "L", L_r)
                if r == 0:
                    pb.inc(v.tensor_scalar(out=bufC[:, 0:1024], in0=bufC[:, 2048:3072],
                                           scalar1=predm[:, r:r + 1], scalar2=None,
                                           op0=ALU.mult, op1=ALU.bypass), "V")
                else:
                    pb.w(v, "V")
                    pb.inc(v.scalar_tensor_tensor(out=bufC[:, 0:1024], in0=bufC[:, 2048:3072],
                                                  scalar=predm[:, r:r + 1], in1=bufC[:, 0:1024],
                                                  op0=ALU.mult, op1=ALU.add), "V")

        @block.gpsimd
        def _(g, ti=ti):
            pb.w(g, "V")
            pb.dma(g, d_vp[ti * P:(ti + 1) * P, :], bufC[:, 0:1024], s="ST")

    for r in range(4):
        @block.sync
        def _(sync, r=r):
            pb.w(sync, "V")
            pb.w(sync, "T")
            pb.dma(sync, bufS[0:16, 0:2048], d_cto[r * 16:(r + 1) * 16, :], s="L")
            pb.dma(sync, bufC[:, 2048:2304].rearrange("p (t f) -> p t f", f=16),
                   d_wgo[r * 2048:(r + 1) * 2048, :].rearrange("(t p) f -> p t f", p=P), s="L")
        L_r = pb.c["L"]

        @block.vector
        def _(v, r=r):
            pb.w(v, "L", L_r)
            pb.inc(v.tensor_scalar(out=bufS[0:16, 0:2048], in0=bufS[0:16, 0:2048],
                                   scalar1=bufS[0:16, 2047:2048], scalar2=-1.0,
                                   op0=ALU.subtract, op1=ALU.mult), "V")
            if r == 0:
                pb.inc(v.tensor_scalar(out=gcols[:, 768:1024], in0=bufC[:, 2048:2304],
                                       scalar1=predm[:, r:r + 1], scalar2=None,
                                       op0=ALU.mult, op1=ALU.bypass), "V")
            else:
                pb.w(v, "V")
                pb.inc(v.scalar_tensor_tensor(out=gcols[:, 768:1024], in0=bufC[:, 2048:2304],
                                              scalar=predm[:, r:r + 1], in1=gcols[:, 768:1024],
                                              op0=ALU.mult, op1=ALU.add), "V")

        for kt in range(16):
            @block.tensor
            def _(t, r=r, kt=kt):
                pb.w(t, "V")
                pb.inc(t.transpose(PS[0:128, 3072 + (kt % 4) * 128: 3072 + (kt % 4) * 128 + 16],
                                   bufS[0:16, kt * 128:(kt + 1) * 128], ident[0:16, 0:16]), "T")

            @block.vector
            def _(v, r=r, kt=kt):
                pb.w(v, "T")
                if r == 0:
                    pb.inc(v.tensor_scalar(out=gcols[:, kt * 16:(kt + 1) * 16],
                                           in0=PS[0:128, 3072 + (kt % 4) * 128: 3072 + (kt % 4) * 128 + 16],
                                           scalar1=predm[:, r:r + 1], scalar2=None,
                                           op0=ALU.mult, op1=ALU.bypass), "V")
                else:
                    pb.w(v, "V")
                    pb.inc(v.scalar_tensor_tensor(out=gcols[:, kt * 16:(kt + 1) * 16],
                                                  in0=PS[0:128, 3072 + (kt % 4) * 128: 3072 + (kt % 4) * 128 + 16],
                                                  scalar=predm[:, r:r + 1], in1=gcols[:, kt * 16:(kt + 1) * 16],
                                                  op0=ALU.mult, op1=ALU.add), "V")

    ksb = lambda s: bufC[:, 2324 + s * 128: 2324 + (s + 1) * 128]
    vsb = lambda s: bufC[:, 2064 + s * 130: 2064 + (s + 1) * 130]
    opair = bufC[:, 2580:3092]
    dn = bufC[0:1, 3092:3604]
    cirow = bufC[0:1, 3092:3604]

    @block.gpsimd
    def _(g):
        pb.w(g, "V")
        for slot in range(2):
            for h2 in range(2):
                g.memset(bufC[:, 2064 + slot * 130 + h2 * 65 + 64: 2064 + slot * 130 + h2 * 65 + 65], 1.0)
        g.sem_inc(pb.sems["G"], 1)
    pb.c["G"] += 1
    G_ones = pb.c["G"]

    for g4 in range(4):
        for hp in range(NCH):
            for h2 in range(2):
                h = 2 * hp + h2

                @block.sync
                def _(sync, g4=g4, hp=hp, h=h, h2=h2):
                    if g4 == 0 and hp == 0 and h2 == 0:
                        pb.w(sync, "G", G_ones)
                        pb.w(sync, "ST")
                    pb.w(sync, "T")
                    pb.dma(sync, cirow[0:1, 0:512], d_ct[h:h + 1, g4 * 512:(g4 + 1) * 512], s="L")
                L_ci = pb.c["L"]

                @block.tensor
                def _(t, g4=g4, h2=h2):
                    pb.w(t, "L", L_ci)
                    pb.w(t, "V")
                    pb.inc(t.matmul(PS[:, 3072:3584], ones[0:1, 0:128], cirow[0:1, 0:512],
                                    start=True, stop=True), "T")

                @block.vector
                def _(v, g4=g4, h2=h2):
                    pb.w(v, "T")
                    pb.inc(v.tensor_copy(bufS[:, h2 * 512:(h2 + 1) * 512], PS[:, 3072:3584]), "V")

            kts = [(True, j) for j in range(4 * g4, 16)] + [(False, j) for j in range(0, 4 * g4 + 4)]
            ki_guard = {}
            t_flash0 = pb.c["T"]
            for ki, (isp, kt) in enumerate(kts):
                slot = ki % 2

                @block.sync
                def _(sync, g4=g4, hp=hp, isp=isp, kt=kt, ki=ki, slot=slot):
                    pb.w(sync, "T", ki_guard.get(ki - 2, t_flash0))
                    ksrc = d_kp if isp else d_kf
                    vsrc = d_vp if isp else d_v
                    pb.dma(sync, ksb(slot), ksrc[hp * 128:(hp + 1) * 128, kt * 128:(kt + 1) * 128], s="L")
                    pb.dma(sync, bass.AP(bufC.ap().tensor, 2064 + slot * 130,
                                         [[4096, 128], [65, 2], [1, 64]]),
                           vsrc[kt * 128:(kt + 1) * 128, hp * 128:(hp + 1) * 128]
                           .rearrange("p (h f) -> p h f", f=64), s="L")
                L_k = pb.c["L"]

                @block.tensor
                def _(t, g4=g4, hp=hp, ki=ki, slot=slot):
                    pb.w(t, "L", L_k)
                    pb.w(t, "V")
                    for h2 in range(2):
                        pb.inc(t.matmul(bank(2 + h2), ksb(slot)[h2 * 64:(h2 + 1) * 64, :],
                                        Qfm[h2 * 64:(h2 + 1) * 64, hp * 2048 + g4 * 512: hp * 2048 + (g4 + 1) * 512],
                                        start=True, stop=True, tile_position=(h2 * 64, 0)), "T")

                masked = (not isp) and (kt * 128 - g4 * 512 >= 0)
                o = kt * 128 - g4 * 512
                for h2 in range(2):
                    h = 2 * hp + h2
                    ncol = gcols[:, (0 if isp else 512) + kt * 16 + h: (0 if isp else 512) + kt * 16 + h + 1]
                    wcol = gcols[:, (768 if isp else 256) + kt * 16 + h: (768 if isp else 256) + kt * 16 + h + 1]

                    if masked:
                        @block.vector
                        def _(v, g4=g4, h2=h2, o=o):
                            pb.w(v, "S")
                            pb.inc(v.tensor_tensor(out=bufS[:, 1024:1536], in0=bufS[:, h2 * 512:(h2 + 1) * 512],
                                                   in1=ccau[:, 384 - o: 384 - o + 512], op=ALU.add), "V")

                        @block.scalar
                        def _(s, g4=g4, h2=h2, ncol=ncol):
                            pb.w(s, "V")
                            pb.w(s, "T")
                            pb.inc(s.activation(out=bufS[:, 1536:2048], in_=bufS[:, 1024:1536],
                                                func=AF.Exp, bias=ncol), "S")
                    else:
                        @block.scalar
                        def _(s, g4=g4, h2=h2, ncol=ncol):
                            pb.w(s, "V")
                            pb.w(s, "T")
                            pb.inc(s.activation(out=bufS[:, 1536:2048], in_=bufS[:, h2 * 512:(h2 + 1) * 512],
                                                func=AF.Exp, bias=ncol), "S")

                    @block.vector
                    def _(v, g4=g4, h2=h2, wcol=wcol):
                        pb.w(v, "S")
                        pb.inc(v.scalar_tensor_tensor(out=bufS[:, 1536:2048], in0=bufS[:, 1536:2048],
                                                      scalar=wcol, in1=PS[:, (2 + h2) * 512:(3 + h2) * 512],
                                                      op0=ALU.mult, op1=ALU.mult), "V")

                    @block.tensor
                    def _(t, g4=g4, hp=hp, h2=h2, ki=ki, slot=slot):
                        pb.w(t, "V")
                        lhs = vsb(slot)[:, h2 * 65:(h2 + 1) * 65]
                        pb.inc(t.matmul(PS[0:65, h2 * 512:(h2 + 1) * 512], lhs,
                                        bufS[:, 1536:2048],
                                        start=(ki == 0), stop=(ki == len(kts) - 1)), "T")
                ki_guard[ki] = pb.c["T"]

            for h2 in range(2):
                @block.vector
                def _(v, g4=g4, hp=hp, h2=h2):
                    pb.w(v, "T")
                    pb.inc(v.tensor_copy(dn[0:1, 0:512], PS[64:65, h2 * 512:(h2 + 1) * 512]), "V")
                    pb.w(v, "V")
                    pb.inc(v.tensor_scalar(out=dn[0:1, 0:512], in0=dn[0:1, 0:512], scalar1=1e-6,
                                           scalar2=None, op0=ALU.add, op1=ALU.bypass), "V")
                    pb.w(v, "V")
                    pb.inc(v.reciprocal(out=dn[0:1, 0:512], in_=dn[0:1, 0:512]), "V")

                @block.tensor
                def _(t, g4=g4, hp=hp, h2=h2):
                    pb.w(t, "V")
                    pb.inc(t.matmul(PS[0:64, 3072:3584], ones[0:1, 0:64], dn[0:1, 0:512],
                                    start=True, stop=True), "T")

                @block.vector
                def _(v, g4=g4, hp=hp, h2=h2):
                    pb.w(v, "T")
                    pb.w(v, "ST")
                    pb.inc(v.tensor_copy(bufS[0:64, 0:512], PS[0:64, 3072:3584]), "V")
                    pb.w(v, "V")
                    pb.inc(v.tensor_tensor(out=opair[h2 * 64:(h2 + 1) * 64, 0:512],
                                           in0=PS[0:64, h2 * 512:(h2 + 1) * 512],
                                           in1=bufS[0:64, 0:512], op=ALU.mult), "V")

            @block.gpsimd
            def _(g, g4=g4, hp=hp):
                pb.w(g, "V")
                pb.dma(g, d_att[hp * 128:(hp + 1) * 128, g4 * 512:(g4 + 1) * 512], opair[:, 0:512], s="ST")

    @block.sync
    def _(sync):
        pb.w(sync, "T")
        pb.dma(sync, Wsb[:].rearrange("p (c f) -> p c f", f=D),
               io["L1_attn_Wo"].rearrange("(c p) f -> p c f", p=P), s="L")
    L_wo = pb.c["L"]
    for g4 in range(4):
        @block.sync
        def _(sync, g4=g4):
            pb.w(sync, "ST")
            if g4 > 0:
                pb.w(sync, "T")
            pb.dma(sync, bufC[:, 0:4096].rearrange("p (c f) -> p c f", f=512),
                   d_att[:, g4 * 512:(g4 + 1) * 512].rearrange("(c p) f -> p c f", p=P), s="L")
        L_att = pb.c["L"]

        for qs in range(4):
            ti = g4 * 4 + qs

            @block.tensor
            def _(t, g4=g4, qs=qs, ti=ti):
                pb.w(t, "L", L_att)
                if g4 == 0 and qs == 0:
                    pb.w(t, "L", L_wo)
                pb.w(t, "V")
                for half in range(2):
                    for c8 in range(NCH):
                        pb.inc(t.matmul(bank(4 + half),
                                        bufC[:, c8 * 512 + qs * 128: c8 * 512 + (qs + 1) * 128],
                                        Wsb[:, c8 * D + half * 512: c8 * D + (half + 1) * 512],
                                        start=(c8 == 0), stop=(c8 == NCH - 1)), "T")

            @block.vector
            def _(v, g4=g4, qs=qs, ti=ti):
                pb.w(v, "T")
                xt = xr[:, ti * D:(ti + 1) * D]
                pb.inc(v.tensor_tensor(out=xt, in0=PS[:, 2048:3072], in1=xt, op=ALU.add), "V")


EMITTERS = {"l0a": emit_l0a, "l0f": emit_l0f, "l1f": emit_l1f, "l2f": emit_l2f,
            "l3f": emit_l3f, "l2a": emit_l2a, "l3a": emit_l3a, "l1a": emit_l1a}

BLOCK_DRAM = {
    "l0a": {"d_agin": (TLOC, D), "d_agout": (SQ, D), "d_enin": (16, 16),
            "d_enout": (16, 16), "d_attn": (4, 16)},
    "l0f": {},
    "l1f": {},
    "l2f": {},
    "l3f": {},
    "l2a": {"d_kfin": (D, TLOC), "d_kfout": (4 * D, TLOC),
            "d_vin": (TLOC, D), "d_vout": (SQ, D)},
    "l1a": {"d_kf": (D, TLOC), "d_v": (TLOC, D), "d_lgt": (16, TLOC), "d_ct": (16, TLOC),
            "d_wg": (TLOC, 16), "d_kfo": (4 * D, TLOC), "d_vo": (SQ, D),
            "d_cto": (64, TLOC), "d_wgo": (SQ, 16), "d_kp": (D, TLOC),
            "d_vp": (TLOC, D), "d_att": (D, TLOC)},
    "l3a": {"d_kf": (D, TLOC), "d_v": (TLOC, D), "d_ktin": (D, P),
            "d_ktout": (4 * D, P), "d_vtin": (P, D), "d_vtout": (4 * P, D),
            "d_att": (D, TLOC)},
}

def ident_t(a):
    return np.ascontiguousarray(a)


def tr_t(a):
    return np.ascontiguousarray(a.T)


def tr_e(a):
    return np.ascontiguousarray(a.transpose(0, 2, 1))


def tr_g1(a):
    return np.ascontiguousarray(a[0].transpose(1, 0, 2).reshape(4, 1024))


def tr_g2(a):
    return np.ascontiguousarray(a[:, :, 0, :].transpose(0, 2, 1).reshape(4, 256))


BLOCK_WEIGHTS = {
    "l0a": {"L0_attn_Wq": ident_t, "L0_attn_Wk": ident_t, "L0_attn_Wv": ident_t,
            "L0_attn_Wo": ident_t, "L0_attn_bo": ident_t},
    "l0f": {"L0_ffn_weight": ident_t, "L0_ffn_bias": ident_t},
    "l1f": {"L1_ffn_G1_R": tr_g1, "L1_ffn_G2_R": tr_g2, "L1_ffn_G1_I": tr_g1,
            "L1_ffn_G2_I": tr_g2, "L1_ffn_bias_real": ident_t, "L1_ffn_bias_imag": ident_t},
    "l2f": {"L2_ffn_w1_w": tr_t, "L2_ffn_w1_s": ident_t, "L2_ffn_w2_w": tr_t,
            "L2_ffn_w2_s": ident_t, "L2_ffn_h1_w": tr_t, "L2_ffn_h1_s": ident_t,
            "L2_ffn_h2_w": tr_t, "L2_ffn_h2_s": ident_t, "L2_ffn_w3_w": tr_t,
            "L2_ffn_w3_s": ident_t},
    "l3f": {"L3_ffn_gate_w": tr_t, "L3_ffn_expert_gate": tr_e, "L3_ffn_expert_up": tr_e,
            "L3_ffn_expert_down": tr_e, "L3_ffn_shared_gate_w": tr_t,
            "L3_ffn_shared_up_w": tr_t, "L3_ffn_shared_down_w": tr_t},
    "l2a": {"L2_attn_Wq": tr_t, "L2_attn_Wk": tr_t, "L2_attn_Wv": tr_t},
    "l3a": {"L3_attn_Wq": tr_t, "L3_attn_Wk": tr_t, "L3_attn_Wv": tr_t, "L3_attn_Wo": tr_t},
    "l1a": {"L1_attn_Wq": tr_t, "L1_attn_Wk": tr_t, "L1_attn_Wv": tr_t, "L1_attn_Wo": tr_t,
            "L1_attn_Ww": tr_t, "L1_attn_bw": ident_t, "L1_attn_We": tr_t, "L1_attn_be": ident_t},
}


def build_launch(blocks, wshapes):
    nc = bass.Bass("TRN2", target_bir_lowering=False)
    io = {}
    io["x_in"] = nc.dram_tensor("x_in", [TLOC, D], F32, kind="ExternalInput").ap()
    io["x_out"] = nc.dram_tensor("x_out", [TLOC, D], F32, kind="ExternalOutput").ap()
    dbg_ap = nc.dram_tensor("dbg_out", [max(NDBG, 1), P, 1024], F32, kind="ExternalOutput").ap() if NDBG else None
    for n, shp in CONST_SHAPES.items():
        io[n] = nc.dram_tensor(n + "_hbm", list(shp), F32, kind="ExternalInput").ap()
    wnames = []
    for b in blocks:
        for wn in BLOCK_WEIGHTS[b]:
            if wn not in io:
                io[wn] = nc.dram_tensor(wn, list(wshapes[wn]), F32, kind="ExternalInput").ap()
                wnames.append(wn)
        for dn, shp in BLOCK_DRAM[b].items():
            io[dn + "@" + b] = nc.dram_tensor(dn + "_" + b, list(shp), F32).ap()

    class ESB:
        pass

    es = ExitStack()
    with es:
        pb = PB.__new__(PB)
        pb.nc = nc
        pb.es = es
        pb.sems = {}
        pb.c = {}
        for name in ["L", "L2", "ST", "T", "V", "S", "G", "CC"]:
            pb.sems[name] = es.enter_context(nc.semaphore("sem_" + name))
            pb.c[name] = 0
        pb.sb = {}
        pb.ps = {}
        pb.dbg_ap = dbg_ap
        alloc_common(pb)
        block = es.enter_context(nc.Block())
        pb.block = block

        emit_load_consts(pb, io)
        emit_load_x(pb, io)
        for b in blocks:
            emit_barrier(pb)
            bio = dict(io)
            for dn in BLOCK_DRAM[b]:
                bio[dn] = io[dn + "@" + b]
            EMITTERS[b](pb, bio)
        emit_store_x(pb, io)
    return nc, wnames


_BUILD_CACHE = {}


def kernel(**inputs):
    consts = [host_consts(c) for c in range(NCORES)]
    x = np.ascontiguousarray(np.asarray(inputs["L0_attn_x"], np.float32))
    xs = x.reshape(2, 4, TLOC, D)

    cur = [np.ascontiguousarray(xs[c // 4, c % 4]) for c in range(NCORES)]
    for blocks in LAUNCH_PLAN:
        wdata = {}
        for b in blocks:
            for wn, tr in BLOCK_WEIGHTS[b].items():
                if wn not in wdata:
                    wdata[wn] = tr(np.ascontiguousarray(np.asarray(inputs[wn], np.float32)))
        key = tuple(blocks)
        if key not in _BUILD_CACHE:
            _BUILD_CACHE[key] = build_launch(blocks, {k: v.shape for k, v in wdata.items()})
        nc, wnames = _BUILD_CACHE[key]
        in_maps = []
        for c in range(NCORES):
            m = {"x_in": cur[c]}
            m.update({k + "_hbm": v for k, v in consts[c].items()})
            m.update(wdata)
            in_maps.append(m)
        res = run_bass_kernel_spmd(nc, in_maps, core_ids=list(range(NCORES)))
        if NDBG:
            global DBG_RESULTS
            DBG_RESULTS = [np.asarray(res.results[c]["dbg_out"]) for c in range(NCORES)]
        cur = [np.asarray(res.results[c]["x_out"]) for c in range(NCORES)]
    out = np.stack([np.concatenate(cur[g * 4:(g + 1) * 4], axis=0) for g in range(2)], axis=0)
    return out.reshape(2, SQ, D).astype(np.float32)
